# version 20
# baseline (speedup 1.0000x reference)
import numpy as np
import ml_dtypes
import concourse.bass as bass
import concourse.mybir as mybir
from concourse.bass_utils import run_bass_kernel_spmd
from contextlib import ExitStack

F32 = mybir.dt.float32
BF16 = mybir.dt.bfloat16
I32 = mybir.dt.int32
AF = mybir.ActivationFunctionType
ALU = mybir.AluOpType
AX = mybir.AxisListType.X

NCORE = 8
SEQ = 2048
D = 1024
NSEQ = 4
ROWS = NSEQ * SEQ
NT = ROWS // 128
NG = 4
EPS = 1e-5


class Ctr:
    __slots__ = ("name", "h", "n")

    def __init__(self, name):
        self.name = name
        self.h = None
        self.n = 0


class Prog:
    ENGS = ("sync", "tensor", "vector", "scalar", "gpsimd")

    def __init__(self):
        self.ops = {e: [] for e in self.ENGS}
        self.ctrs = []

    def ctr(self, name):
        c = Ctr(name)
        self.ctrs.append(c)
        return c

    CHAINED = ("vector", "scalar", "gpsimd")

    def op(self, eng, fn, waits=(), inc=None, inc_n=1):
        w = []
        for x in waits:
            if x is None:
                continue
            if isinstance(x, list):
                w += [y for y in x if y is not None]
            else:
                w.append(x)
        if eng in self.CHAINED:
            c = self.chain[eng]
            if c.n > 0:
                w.append((c, c.n))
            if inc_n == 16:
                pass
            elif inc is None:
                inc = c
            elif inc is not c:
                inc = c
        ev = None
        if inc is not None:
            inc.n += inc_n
            ev = (inc, inc.n)
        self.ops[eng].append((fn, w, inc, inc_n))
        return ev

    def dma(self, eng, fn, waits=(), inc=None):
        return self.op(eng, fn, waits, inc, 16)

    def stream(self, name, k=4):
        return DmaStream(self, name, k)

    def emit(self, block):
        def mk(eng_name):
            def run(eng):
                hi = {}
                for fn, waits, inc, inc_n in self.ops[eng_name]:
                    for c, v in waits:
                        if hi.get(c.name, 0) >= v:
                            continue
                        eng.wait_ge(c.h, v)
                        hi[c.name] = v
                    inst = fn(eng)
                    if inc is not None:
                        inst.then_inc(inc.h, inc_n)
            return run

        for name in self.ENGS:
            if self.ops[name]:
                getattr(block, name)(mk(name))


class DmaStream:

    def __init__(self, P, name, k):
        self.P = P
        self.ctrs = [P.ctr(f"{name}{j}") for j in range(k)]
        self.k = k
        self.idx = 0

    def dma(self, eng, fn, waits=()):
        j = self.idx % self.k
        rnd = self.idx // self.k
        self.idx += 1
        c = self.ctrs[j]
        w = list(waits) + ([(c, 16 * rnd)] if rnd > 0 else [])
        self.P.op(eng, fn, w, c, 16)
        return (c, 16 * (rnd + 1))

    def all_done(self):
        evs = []
        for j, c in enumerate(self.ctrs):
            n = (self.idx - j + self.k - 1) // self.k
            if n > 0:
                evs.append((c, 16 * n))
        return evs


def build_program(debug_taps=()):
    nc = bass.Bass("TRN2", target_bir_lowering=False, dynamic_dma_scratch_size=128)

    x_in = nc.dram_tensor("x_in", [ROWS, D], F32, kind="ExternalInput").ap()
    wq_h = nc.dram_tensor("wq_h", [D, D], F32, kind="ExternalInput").ap()
    wk_h = nc.dram_tensor("wk_h", [D, D], F32, kind="ExternalInput").ap()
    wv_h = nc.dram_tensor("wv_h", [D, D], F32, kind="ExternalInput").ap()
    wo_h = nc.dram_tensor("wo_h", [D, D], F32, kind="ExternalInput").ap()
    fw_h = nc.dram_tensor("fw_h", [D, D], F32, kind="ExternalInput").ap()
    wob_h = nc.dram_tensor("wob_h", [128, D], BF16, kind="ExternalInput").ap()
    ffb_h = nc.dram_tensor("ffb_h", [128, D], BF16, kind="ExternalInput").ap()
    ident_h = nc.dram_tensor("ident_h", [128, 128], BF16, kind="ExternalInput").ap()
    identf_h = nc.dram_tensor("identf_h", [128, 128], F32, kind="ExternalInput").ap()
    swap_h = nc.dram_tensor("swap_h", [128, 128], BF16, kind="ExternalInput").ap()
    tri_h = nc.dram_tensor("tri_h", [128, 128], BF16, kind="ExternalInput").ap()
    ropec_h = nc.dram_tensor("ropec_h", [128, SEQ], BF16, kind="ExternalInput").ap()
    ropes_h = nc.dram_tensor("ropes_h", [128, SEQ], BF16, kind="ExternalInput").ap()
    out_h = nc.dram_tensor("out_h", [ROWS, D], F32, kind="ExternalOutput").ap()

    x2_h = nc.dram_tensor("x2_h", [ROWS, D], F32).ap()
    flip_h = nc.dram_tensor("flip_h", [128], F32).ap()
    flip_pc = flip_h.rearrange("(p o) -> p o", o=1)
    flip_rc = flip_h.rearrange("(o p) -> o p", o=1)
    cc1i = nc.dram_tensor("cc1i", [1, 8], F32)
    cc1o = nc.dram_tensor("cc1o", [1, 8], F32)
    cc2i = nc.dram_tensor("cc2i", [1, 8], F32)
    cc2o = nc.dram_tensor("cc2o", [1, 8], F32)

    taps = {}
    for t in debug_taps:
        if t in ("x1", "x2", "qin1"):
            taps[t] = nc.dram_tensor("tap_" + t, [ROWS, D], F32, kind="ExternalOutput").ap()
        elif t in ("kt", "qt"):
            taps[t] = nc.dram_tensor("tap_" + t, [D, SEQ], F32, kind="ExternalOutput").ap()
        elif t == "attn":
            taps[t] = nc.dram_tensor("tap_attn", [D, SEQ], F32, kind="ExternalOutput").ap()

    P = Prog()
    s_ld = P.stream("ld", 4)
    s_ld2 = P.stream("ld2", 4)
    s_gl = P.stream("gl", 4)
    s_st = P.stream("st", 4)
    s_st2 = P.stream("st2", 4)
    s_wt = P.stream("wt", 4)

    class S_LD:
        @staticmethod
        def dma(eng, fn, waits=()):
            return s_ld.dma(eng, fn, waits)

    class S_GP:
        @staticmethod
        def dma(eng, fn, waits=(), _st=False):
            if _st:
                return s_st.dma("sync", fn, waits)
            return s_gl.dma("sync", fn, waits)
    c_t = P.ctr("t")
    c_v = P.ctr("v")
    c_s = P.ctr("s")
    c_cc = P.ctr("cc")
    c_g = P.ctr("g")
    P.chain = {"vector": c_v, "scalar": c_s, "gpsimd": c_g}

    with ExitStack() as ctx:
        e = ctx.enter_context

        def sb(name, shape, dt):
            return e(nc.sbuf_tensor(name, shape, dt))

        wq_sb = sb("wq_sb", [128, 8 * D], BF16)
        wk_sb = sb("wk_sb", [128, 8 * D], BF16)
        wv_sb = sb("wv_sb", [128, 8 * D], BF16)
        wo_sb = sb("wo_sb", [128, 8 * D], BF16)
        qwt_sb = sb("qwt_sb", [128, 8 * D], BF16)
        ktb = sb("ktb", [128, 8 * SEQ], BF16)
        vtb = sb("vtb", [128, 16 * D], BF16)
        qtb = sb("qtb", [128, 8 * 512], BF16)
        xn2t = sb("xn2t", [128, 4 * D], BF16)
        xbuf = sb("xbuf", [128, 4 * D], F32)
        pbuf = sb("pbuf", [128, 2 * D], BF16)
        att = sb("att", [128, 8 * 512], BF16)
        qint = att
        ropex = pbuf[:, 0:512]
        ropet2 = pbuf[:, 512:1024]
        tq = sb("tq", [128, D], F32)
        q32 = sb("q32", [128, D // 2], I32)
        qout = sb("qout", [128, D // 2], F32)
        epi = sb("epi", [128, D], F32)
        rsc2 = sb("rsc2", [128, D], F32)

        identf = sb("identf", [128, 128], F32)
        swapt = sb("swapt", [128, 128], BF16)
        trit = sb("trit", [128, 128], BF16)
        ropec = sb("ropec", [128, SEQ], BF16)
        ropes = sb("ropes", [128, SEQ], BF16)
        ones64 = sb("ones64", [128, 64], BF16)
        onesrow = sb("onesrow", [1, 128], F32)
        onescol = sb("onescol", [128, 1], F32)
        ones1b = sb("ones1b", [1, 128], BF16)
        fbrow = sb("fbrow", [1, 2048], BF16)
        flip_row = sb("flip_row", [1, 128], F32)
        st1 = sb("st1", [128, NT], F32)
        sq1 = sb("sq1", [128, NT], F32)
        rmx = sb("rmx", [128, NT], F32)
        rmn = sb("rmn", [128, NT], F32)
        m1 = sb("m1", [128, NT], F32)
        r1s = sb("r1s", [128, NT], F32)
        b1s = sb("b1s", [128, NT], F32)
        st3, sq3, rmx3, rmn3, m3 = st1, sq1, rmx, rmn, m1
        r3s = sb("r3s", [128, NT], F32)
        b3s = sb("b3s", [128, NT], F32)
        sc = sb("sc", [128, 8], F32)
        scr = sb("scr", [1, 16], F32)
        wsum = sb("wsum", [128, 8], F32)
        m2t = sb("m2t", [128, 4], F32)
        r2t = sb("r2t", [128, 4], F32)
        b2t = sb("b2t", [128, 4], F32)
        v2t = sb("v2t", [128, 4], F32)
        colscr = sb("colscr", [128, 8], F32)
        rs128 = colscr[:, 0:1]
        ch128 = colscr[:, 1:2]
        rs2128 = colscr[:, 2:3]
        ch2128 = colscr[:, 3:4]
        thr128 = colscr[:, 4:5]

        psA = e(nc.psum_tensor("psA", [128, 1024], F32))
        psB = e(nc.psum_tensor("psB", [128, 2048], F32))
        psC = e(nc.psum_tensor("psC", [128, 512], F32))
        psD = e(nc.psum_tensor("psD", [128, 512], F32))
        psT = psB[:, 1024:2048]

        for c in P.ctrs:
            c.h = e(nc.semaphore(c.name))
        block = e(nc.Block())

        ev_flip_free = None

        def bcast128(dst_col, src11, waits):
            nonlocal ev_flip_free
            ev1 = P.op("vector",
                       lambda v, s=src11: v.tensor_scalar(
                           flip_row[0:1, :], onesrow[0:1, :], s, None, ALU.mult),
                       waits=list(waits) + [ev_flip_free], inc=c_v)
            ev2 = S_GP.dma("gpsimd", lambda g: g.dma_start(out=flip_rc, in_=flip_row[0:1, :]),
                        waits=[ev1])
            ev3 = S_GP.dma("gpsimd", lambda g, d=dst_col: g.dma_start(out=d, in_=flip_pc),
                        waits=[ev2])
            ev_flip_free = ev3
            return ev3

        def pmax_flip(src_col, dst11, waits):
            nonlocal ev_flip_free
            ev1 = S_GP.dma("gpsimd", lambda g, s=src_col: g.dma_start(out=flip_pc, in_=s),
                        waits=list(waits) + [ev_flip_free])
            ev2 = S_GP.dma("gpsimd", lambda g: g.dma_start(out=flip_row[0:1, :], in_=flip_rc),
                        waits=[ev1])
            ev3 = P.op("vector",
                       lambda v, d=dst11: v.reduce_max(d, flip_row[0:1, :], axis=AX),
                       waits=[ev2], inc=c_v)
            ev_flip_free = ev3
            return ev3

        psT_free = {"p2": None, "p4o": None}

        def transpose_tile(src_ap, dst_sb, dst_off, wa, identt=None, ev_idw=None,
                           key="p2", regions=None):
            idt = identf if identt is None else identt
            r = regions if regions is not None else (psT[:, 0:512], psT[:, 512:1024])
            evt = None
            for kc in range(8):
                w = list(wa) if kc == 0 else []
                if kc == 0:
                    w.append(psT_free[key])
                    w.append(ev_idw)
                rr = r[kc // 4]
                co = 128 * (kc % 4)
                evt = P.op("tensor",
                           lambda t, kc=kc, rr=rr, co=co, idt=idt: t.transpose(
                               rr[:, co:co + 128],
                               src_ap[:, 128 * kc:128 * (kc + 1)], idt[:]),
                           waits=w, inc=(c_t if kc == 7 else None))
            evc1 = P.op("vector",
                        lambda v, dst_sb=dst_sb, dst_off=dst_off, r=r: v.tensor_copy(
                            dst_sb[:, dst_off:dst_off + 512], r[0][:, 0:512]),
                        waits=[evt], inc=c_v)
            evc2 = P.op("scalar",
                        lambda sc_, dst_sb=dst_sb, dst_off=dst_off, r=r: sc_.copy(
                            dst_sb[:, dst_off + 512:dst_off + D], r[1][:, 0:512]),
                        waits=[evt], inc=c_s)
            psT_free[key] = [evc1, evc2]
            return [evc1, evc2]

        ev_qinb_free = None
        ev_tq_free = [None, None]

        def quant_tile(sl, rS, bS, wa, src=None):
            nonlocal ev_qinb_free
            H2 = D // 2
            evg = [None, None]
            for hh in range(2):
                ho = hh * H2
                sap = xbuf[:, sl * D + ho:sl * D + ho + H2] if src is None \
                    else src[:, ho:ho + H2]
                eva = P.op("scalar",
                           lambda s, sap=sap, rS=rS, bS=bS, ho=ho: s.activation(
                               tq[:, ho:ho + H2], sap, AF.Identity,
                               scale=rS, bias=bS),
                           waits=list(wa) + [ev_qinb_free, ev_tq_free[hh]], inc=c_s)
                P.op("vector",
                     lambda v, ho=ho: v.tensor_copy(q32[:], tq[:, ho:ho + H2]),
                     waits=[eva], inc=None)
                P.op("vector",
                     lambda v: v.tensor_copy(qout[:, 0:H2], q32[:]),
                     inc=None)
                P.op("vector",
                     lambda v, ho=ho: v.tensor_tensor(
                         tq[:, ho:ho + H2], tq[:, ho:ho + H2],
                         qout[:, 0:H2], ALU.is_ge),
                     inc=None)
                ev4 = P.op("vector",
                           lambda v, ho=ho: v.scalar_tensor_tensor(
                               tq[:, ho:ho + H2], tq[:, ho:ho + H2], -257.0,
                               qout[:, 0:H2], ALU.add, ALU.add),
                           inc=c_v)
                ev_tq_free[hh] = ev4
                evg[hh] = ev4
            return [evg[0], evg[1]]

        ev_const = []
        for dst, src in ((swapt, swap_h), (trit, tri_h),
                         (ropec, ropec_h), (ropes, ropes_h),
                         (None, wob_h), (None, ffb_h), (identf, identf_h)):
            if dst is None:
                off = 1024 if src is wob_h else 0
                ev_const.append(S_LD.dma("sync",
                                      lambda s, r=src, off=off: s.dma_start(
                                          out=fbrow[0:1, off:off + 1024],
                                          in_=r[0:1, :])))
            else:
                ev_const.append(S_LD.dma("sync",
                                      lambda s, d=dst, r=src: s.dma_start(out=d[:], in_=r)))
        ev_idf = ev_const[6]
        ev_ones = P.op("vector", lambda v: v.memset(ones64[:], 1.0), inc=c_v)
        ev_ones1b = P.op("vector", lambda v: v.memset(ones1b[:], 1.0), inc=c_v)
        P.op("vector", lambda v: v.memset(onesrow[:], 1.0), inc=c_v)
        ev_onescol = P.op("vector", lambda v: v.memset(onescol[:], 1.0), inc=c_v)
        ev_prev_acc = None
        ev_fw1 = []
        for i in range(8):
            evl = S_LD.dma("scalar",
                        lambda s, i=i: s.dma_start(out=tq[:],
                                                   in_=fw_h[128 * i:128 * (i + 1), :]),
                        waits=[ev_prev_acc])
            eva = P.op("scalar",
                       lambda s, i=i: s.activation(epi[:], tq[:], AF.Abs,
                                                   accum_out=wsum[:, i:i + 1]),
                       waits=[evl], inc=c_s)
            ev_prev_acc = eva
            ev_fw1.append(eva)
        ev_ws0 = P.op("vector", lambda v: v.reduce_sum(sc[:, 0:1], wsum[:], axis=AX),
                      waits=ev_fw1, inc=c_v)
        ev_ws1 = P.op("tensor",
                      lambda t: t.matmul(psB[0:1, 0:1], sc[:, 0:1], onescol[:],
                                         start=True, stop=True),
                      waits=[ev_ws0, ev_onescol], inc=c_t)
        ev_ws2 = P.op("vector",
                      lambda v: v.tensor_scalar(scr[0:1, 0:1], psB[0:1, 0:1],
                                                1.0 / float(D * D), None, ALU.mult),
                      waits=[ev_ws1], inc=c_v)
        ev_thr = P.op("vector",
                      lambda v: v.tensor_scalar(scr[0:1, 1:2], scr[0:1, 0:1],
                                                0.5, None, ALU.mult),
                      waits=[ev_ws2], inc=c_v)
        ev_thrb = bcast128(thr128[:, 0:1], scr[0:1, 1:2], [ev_thr])

        ev_prev = None
        ev_prevq = None
        ev_prevt = None
        for i in range(8):
            evl = S_LD.dma("scalar",
                        lambda s, i=i: s.dma_start(out=tq[:],
                                                   in_=fw_h[128 * i:128 * (i + 1), :]),
                        waits=[ev_fw1[-1] if i == 0 else ev_prev])
            eva = P.op("scalar", lambda s: s.activation(epi[:], tq[:], AF.Abs),
                       waits=[evl, ev_thrb, ev_prevq], inc=c_s)
            evm = P.op("vector",
                       lambda v: v.tensor_single_scalar(epi[:], epi[:], thr128[:, 0:1],
                                                        ALU.is_gt),
                       waits=[eva], inc=c_v)
            evsg = P.op("scalar", lambda s: s.activation(pbuf[:, 0:D], tq[:], AF.Sign),
                        waits=[evl, ev_prevt], inc=c_s)
            ev_prev = evsg
            evq = P.op("vector",
                       lambda v: v.tensor_mul(xbuf[:, 2 * D:3 * D], epi[:], pbuf[:, 0:D]),
                       waits=[evm, evsg], inc=c_v)
            ev_prevq = evq
            evt = None
            for kc in range(8):
                w = [evq, ev_idf] if kc == 0 else []
                if kc == 0:
                    w.append(psT_free["p2"])
                evt = P.op("tensor",
                           lambda t, kc=kc: t.transpose(
                               psT[:, 128 * kc:128 * (kc + 1)],
                               xbuf[:, 2 * D + 128 * kc:2 * D + 128 * (kc + 1)],
                               identf[:]),
                           waits=w, inc=(c_t if kc == 7 else None))
            ev_prevt = evt
            evac = None
            for kc in range(8):
                evac = P.op("vector",
                            lambda v, kc=kc, i=i: v.tensor_copy(
                                qwt_sb[:, kc * D + 128 * i: kc * D + 128 * (i + 1)],
                                psT[:, 128 * kc:128 * (kc + 1)]),
                            waits=[evt] if kc == 0 else [], inc=(c_v if kc == 7 else None))
            psT_free["p2"] = evac

        ev_qinb_free = psT_free["p2"]

        ev_w = {}
        ev_wstage = {2: ev_qinb_free, 3: ev_qinb_free}
        for nm, dst, src in (("wv", wv_sb, wv_h), ("wq", wq_sb, wq_h),
                             ("wk", wk_sb, wk_h), ("wo", wo_sb, wo_h)):
            ceng = "vector" if nm in ("wv", "wq") else "gpsimd"
            for i in range(8):
                wsl = 2 + (i % 2)
                qeng = "sync" if i % 2 == 0 else "scalar"
                evwl = s_wt.dma(
                    qeng,
                    lambda g, r=src, i=i, wsl=wsl: g.dma_start(
                        out=xbuf[:, wsl * D:(wsl + 1) * D],
                        in_=r[128 * i:128 * (i + 1), :]),
                    waits=[ev_wstage[wsl]])
                evc = P.op(ceng,
                           lambda g_, d=dst, i=i, wsl=wsl: g_.tensor_copy(
                               d[:, i * D:(i + 1) * D], xbuf[:, wsl * D:(wsl + 1) * D]),
                           waits=[evwl], inc=(c_v if ceng == "vector" else c_g))
                ev_wstage[wsl] = evc
                ev_w[(nm, i)] = evc

        ev_use = [None, None]
        evs = evv = None

        for i in range(NT):
            sl = i % 2
            evl = s_ld.dma("sync",
                        lambda s, i=i, sl=sl: s.dma_start(
                            out=xbuf[:, sl * D:(sl + 1) * D],
                            in_=x_in[128 * i:128 * (i + 1), :]),
                        waits=[ev_use[sl]])
            evid = P.op("scalar",
                        lambda s, i=i, sl=sl: s.activation(
                            psA[:], xbuf[:, sl * D:(sl + 1) * D], AF.Identity,
                            accum_out=st1[:, i:i + 1]),
                        waits=[evl, evs], inc=c_s)
            evs = P.op("scalar",
                       lambda s, i=i, sl=sl: s.activation(
                           psA[:], xbuf[:, sl * D:(sl + 1) * D], AF.Square,
                           accum_out=sq1[:, i:i + 1]),
                       waits=[evid], inc=c_s)
            P.op("vector",
                 lambda v, i=i, sl=sl: v.reduce_max(
                     rmx[:, i:i + 1], xbuf[:, sl * D:(sl + 1) * D], axis=AX),
                 waits=[evl], inc=None)
            evv = P.op("vector",
                       lambda v, i=i, sl=sl: v.tensor_reduce(
                           rmn[:, i:i + 1], xbuf[:, sl * D:(sl + 1) * D], AX, ALU.min),
                       waits=[evl], inc=c_v)
            ev_use[sl] = [evs, evv]

        evm1 = P.op("vector", lambda v: v.tensor_scalar(m1[:], st1[:], 1.0 / D, None, ALU.mult),
                    waits=[evs, evv], inc=c_v)
        P.op("vector", lambda v: v.tensor_scalar(sq1[:], sq1[:], 1.0 / D, None, ALU.mult),
             inc=None)
        P.op("vector", lambda v: v.scalar_tensor_tensor(st1[:], m1[:], -1.0, m1[:],
                                                        ALU.mult, ALU.mult), inc=None)
        P.op("vector", lambda v: v.tensor_add(st1[:], sq1[:], st1[:]), inc=None)
        evv3 = P.op("vector", lambda v: v.tensor_scalar(st1[:], st1[:], EPS, None, ALU.add),
                    inc=c_v)
        evln = P.op("scalar", lambda s: s.activation(sq1[:], st1[:], AF.Ln),
                    waits=[evv3], inc=c_s)
        evr1 = P.op("scalar", lambda s: s.activation(sq1[:], sq1[:], AF.Exp, scale=-0.5),
                    waits=[evln], inc=c_s)
        P.op("vector", lambda v: v.tensor_sub(rmx[:], rmx[:], m1[:]), waits=[evm1], inc=None)
        P.op("vector", lambda v: v.tensor_sub(rmn[:], m1[:], rmn[:]), inc=None)
        P.op("vector", lambda v: v.tensor_max(rmx[:], rmx[:], rmn[:]), inc=None)
        P.op("vector", lambda v: v.tensor_mul(rmx[:], rmx[:], sq1[:]), waits=[evr1], inc=None)
        eva5 = P.op("vector", lambda v: v.reduce_max(sc[:, 1:2], rmx[:], axis=AX), inc=c_v)
        evgm = pmax_flip(sc[:, 1:2], scr[0:1, 2:3], [eva5])
        evrep = P.op("vector",
                     lambda v: v.tensor_scalar(flip_row[0:1, 0:8], onesrow[0:1, 0:8],
                                               scr[0:1, 2:3], None, ALU.mult),
                     waits=[evgm], inc=c_v)
        evci = S_GP.dma("gpsimd", lambda g: g.dma_start(out=cc1i.ap(), in_=flip_row[0:1, 0:8]),
                     waits=[evrep])
        evcc = P.op("gpsimd",
                    lambda g: g.collective_compute(
                        "AllReduce", ALU.max, replica_groups=[list(range(NCORE))],
                        ins=[cc1i.ap()], outs=[cc1o.ap()]),
                    waits=[evci], inc=c_cc)
        evco = S_GP.dma("gpsimd", lambda g: g.dma_start(out=flip_row[0:1, 0:8], in_=cc1o.ap()),
                     waits=[evcc])
        evis = P.op("vector", lambda v: v.tensor_scalar(scr[0:1, 3:4], flip_row[0:1, 0:1],
                                                        1.0 / 127.0, None, ALU.mult),
                    waits=[evco], inc=c_v)
        evrs = P.op("vector", lambda v: v.reciprocal(scr[0:1, 4:5], scr[0:1, 3:4]),
                    waits=[evis], inc=c_v)
        ev_rsb = bcast128(rs128[:, 0:1], scr[0:1, 4:5], [evrs])
        evch = P.op("vector", lambda v: v.tensor_mul(scr[0:1, 5:6], scr[0:1, 0:1],
                                                     scr[0:1, 3:4]),
                    waits=[evis, ev_ws2], inc=c_v)
        evch2 = P.op("vector", lambda v: v.tensor_scalar(scr[0:1, 5:6], scr[0:1, 5:6],
                                                         0.5, None, ALU.mult),
                     waits=[evch], inc=c_v)
        ev_chb = bcast128(ch128[:, 0:1], scr[0:1, 5:6], [evch2])
        evfb1a = P.op("vector", lambda v: v.reciprocal(scr[0:1, 12:13], scr[0:1, 5:6]),
                      waits=[evch2], inc=c_v)
        P.op("vector", lambda v: v.tensor_scalar(scr[0:1, 12:13], scr[0:1, 12:13],
                                                 0.5, None, ALU.mult), inc=None)
        ev_fb1 = P.op("vector",
                      lambda v: v.tensor_single_scalar(fbrow[0:1, 0:1024],
                                                       fbrow[0:1, 0:1024],
                                                       scr[0:1, 12:13], ALU.mult),
                      waits=[evfb1a, ev_const[5], ev_ones1b], inc=c_v)
        evr1s = P.op("vector",
                     lambda v: v.tensor_single_scalar(r1s[:], sq1[:], rs128[:, 0:1], ALU.mult),
                     waits=[ev_rsb, evr1], inc=c_v)
        evb1s = P.op("vector",
                     lambda v: v.scalar_tensor_tensor(b1s[:], m1[:], -1.0, r1s[:],
                                                      ALU.mult, ALU.mult),
                     waits=[evr1s], inc=c_v)
        evb1s = P.op("vector",
                     lambda v: v.tensor_single_scalar(b1s[:], b1s[:], 256.5, ALU.add),
                     inc=c_v)


        ev_acc_prev = evr1
        ev_wqs = P.op("vector",
                      lambda v: v.tensor_scalar(wq_sb[:], wq_sb[:], 0.125, None, ALU.mult),
                      waits=[ev_w[("wq", 7)]], inc=c_v)

        ev_tapx1 = [None] * 4
        ev_xfree = [evb1s, evb1s, [evb1s, ev_qinb_free, ev_wstage[2]],
                    [evb1s, ev_qinb_free, ev_wstage[3]]]
        ev_qtb_free = None
        ev_psA_free = None
        ev_psB_free = [ev_ws2]
        ev_free_CD = None
        ev_free_A01 = None
        ev_rsc_free = {0: None, 1: None}
        ev_pbuf_free = [None, None]
        ev_ktile = {}

        def ffn_mm(evq, fb_off=0, ev_fb=None, acc=None, acc_wait=None,
                   tkey="p2", tregions=None):
            nonlocal ev_psA_free, ev_qinb_free
            if acc is None:
                a0, a1 = psA[:, 0:512], psA[:, 512:1024]
            else:
                a0, a1 = acc
            aw = [ev_psA_free] if acc_wait is None else list(acc_wait)
            evqt = transpose_tile(tq, qint, 0, [evq], identf, ev_idf,
                                  key=tkey, regions=tregions)
            ev_qinb_free = evqt
            for kc in range(8):
                w = ([evqt] + aw) if kc == 0 else []
                P.op("tensor",
                     lambda t, kc=kc, a0=a0: t.matmul(
                         a0, qint[:, 128 * kc:128 * (kc + 1)],
                         qwt_sb[:, kc * D: kc * D + 512],
                         start=(kc == 0), stop=False),
                     waits=w, inc=None)
                P.op("tensor",
                     lambda t, kc=kc, a1=a1: t.matmul(
                         a1, qint[:, 128 * kc:128 * (kc + 1)],
                         qwt_sb[:, kc * D + 512: kc * D + 1024],
                         start=(kc == 0), stop=False),
                     waits=[], inc=None)
            P.op("tensor",
                 lambda t, fb_off=fb_off, a0=a0: t.matmul(
                     a0, ones1b[0:1, :], fbrow[0:1, fb_off:fb_off + 512],
                     start=False, stop=True),
                 waits=[ev_fb], inc=None)
            evmm = P.op("tensor",
                        lambda t, fb_off=fb_off, a1=a1: t.matmul(
                            a1, ones1b[0:1, :],
                            fbrow[0:1, fb_off + 512:fb_off + 1024],
                            start=False, stop=True),
                        waits=[], inc=c_t)
            return evmm

        def ffn_epi(sl, chcol, evmm, extra=(), src=None, xw=()):
            nonlocal ev_psA_free
            eve1 = P.op("scalar",
                        lambda s, chcol=chcol: s.activation(
                            epi[:], psA[:], AF.Identity, scale=chcol),
                        waits=[evmm, ev_chb] + list(extra), inc=c_s)
            ev_psA_free = eve1
            xs = xbuf[:, sl * D:(sl + 1) * D] if src is None else src
            evx1 = P.op("gpsimd",
                        lambda g_, sl=sl, xs=xs: g_.tensor_add(
                            xbuf[:, sl * D:(sl + 1) * D], epi[:], xs),
                        waits=[eve1] + list(xw), inc=c_g)
            return evx1

        for s_i in range(NSEQ):
            ev_vproj = [None] * 16
            for g in range(NG):
                gbase = s_i * 16 + 4 * g
                ev_xn2t = [None] * 4
                evl0 = s_ld.dma("sync",
                                lambda s, gt=gbase: s.dma_start(
                                    out=rsc2[:, 0:D],
                                    in_=x_in[128 * gt:128 * (gt + 1), :]),
                                waits=[ev_rsc_free[0], ev_rsc_free[1], evb1s])
                ev_q = quant_tile(0, r1s[:, gbase:gbase + 1],
                                  b1s[:, gbase:gbase + 1], [evl0], src=rsc2)
                for tau in range(4):
                    gt = gbase + tau
                    sl = tau
                    evmm = ffn_mm(ev_q, 0, ev_fb1)
                    if tau < 3:
                        evl = s_ld.dma("sync",
                                       lambda s, gt=gt + 1, sl=sl + 1: s.dma_start(
                                           out=xbuf[:, sl * D:(sl + 1) * D],
                                           in_=x_in[128 * gt:128 * (gt + 1), :]),
                                       waits=[ev_xfree[sl + 1], evb1s])
                        ev_q = quant_tile(sl + 1, r1s[:, gt + 1:gt + 2],
                                          b1s[:, gt + 1:gt + 2], [evl])
                    if tau == 0:
                        evx1 = ffn_epi(sl, ch128[:, 0:1], evmm,
                                       src=rsc2[:, 0:D], xw=[ev_xfree[0]])
                        ev_rsc_free[0] = evx1
                        ev_rsc_free[1] = evx1
                    else:
                        evx1 = ffn_epi(sl, ch128[:, 0:1], evmm)
                    if "x1" in taps:
                        ev_tapx1[sl] = S_GP.dma(
                            "gpsimd", lambda g_, gt=gt, sl=sl: g_.dma_start(
                                out=taps["x1"][128 * gt:128 * (gt + 1), :],
                                in_=xbuf[:, sl * D:(sl + 1) * D]),
                            waits=[evx1])
                    ev21 = P.op("vector",
                                lambda v, sl=sl, tau=tau: v.reduce_sum(
                                    m2t[:, tau:tau + 1],
                                    xbuf[:, sl * D:(sl + 1) * D], axis=AX),
                                waits=[evx1], inc=c_v)
                    ev22 = P.op("scalar",
                                lambda s, sl=sl, tau=tau: s.activation(
                                    psB[:, 0:1024], xbuf[:, sl * D:(sl + 1) * D],
                                    AF.Square, accum_out=v2t[:, tau:tau + 1]),
                                waits=[evx1, ev_acc_prev], inc=c_s)
                    ev_acc_prev = ev22
                    P.op("vector",
                         lambda v, tau=tau: v.tensor_scalar(
                             m2t[:, tau:tau + 1], m2t[:, tau:tau + 1], 1.0 / D, None,
                             ALU.mult),
                         waits=[ev22, ev21], inc=None)
                    P.op("vector",
                         lambda v, tau=tau: v.tensor_scalar(
                             v2t[:, tau:tau + 1], v2t[:, tau:tau + 1], 1.0 / D, None,
                             ALU.mult), inc=None)
                    P.op("vector",
                         lambda v, tau=tau: v.scalar_tensor_tensor(
                             r2t[:, tau:tau + 1], m2t[:, tau:tau + 1], -1.0,
                             m2t[:, tau:tau + 1], ALU.mult, ALU.mult), inc=None)
                    P.op("vector",
                         lambda v, tau=tau: v.tensor_add(
                             v2t[:, tau:tau + 1], v2t[:, tau:tau + 1],
                             r2t[:, tau:tau + 1]), inc=None)
                    ev2v = P.op("vector",
                                lambda v, tau=tau: v.tensor_scalar(
                                    v2t[:, tau:tau + 1], v2t[:, tau:tau + 1], EPS, None,
                                    ALU.add), inc=c_v)
                    ev2l = P.op("scalar",
                                lambda s, tau=tau: s.activation(
                                    r2t[:, tau:tau + 1], v2t[:, tau:tau + 1], AF.Ln),
                                waits=[ev2v], inc=c_s)
                    ev2r = P.op("scalar",
                                lambda s, tau=tau: s.activation(
                                    r2t[:, tau:tau + 1], r2t[:, tau:tau + 1], AF.Exp,
                                    scale=-0.5),
                                waits=[ev2l], inc=c_s)
                    ev2b = P.op("vector",
                                lambda v, tau=tau: v.scalar_tensor_tensor(
                                    b2t[:, tau:tau + 1], m2t[:, tau:tau + 1], -1.0,
                                    r2t[:, tau:tau + 1], ALU.mult, ALU.mult),
                                waits=[ev2r], inc=c_v)
                    evn2 = P.op("scalar",
                                lambda s, sl=sl, tau=tau: s.activation(
                                    epi[:], xbuf[:, sl * D:(sl + 1) * D], AF.Identity,
                                    scale=r2t[:, tau:tau + 1],
                                    bias=b2t[:, tau:tau + 1]),
                                waits=[ev2b], inc=c_s)
                    evnt = transpose_tile(epi, xn2t, tau * D, [evn2], identf, ev_idf)
                    ev_xn2t[tau] = evnt
                    j = 4 * g + tau
                    evvm = None
                    for kc in range(8):
                        w = [evnt, ev_psA_free] if kc == 0 else []
                        P.op("tensor",
                             lambda t, kc=kc, tau=tau: t.matmul(
                                 psA[:, 0:512],
                                 xn2t[:, tau * D + 128 * kc: tau * D + 128 * (kc + 1)],
                                 wv_sb[:, kc * D: kc * D + 512],
                                 start=(kc == 0), stop=(kc == 7)),
                             waits=w + ([ev_w[("wv", 7)]] if kc == 0 else []), inc=None)
                        evvm = P.op("tensor",
                                    lambda t, kc=kc, tau=tau: t.matmul(
                                        psA[:, 512:1024],
                                        xn2t[:, tau * D + 128 * kc: tau * D + 128 * (kc + 1)],
                                        wv_sb[:, kc * D + 512: kc * D + 1024],
                                        start=(kc == 0), stop=(kc == 7)),
                                    waits=[], inc=(c_t if kc == 7 else None))
                    evve = P.op("scalar",
                                lambda sc_, j=j: sc_.copy(vtb[:, j * D:(j + 1) * D],
                                                          psA[:]),
                                waits=[evvm], inc=c_s)
                    ev_psA_free = evve
                    ev_vproj[j] = evve

                def proj_rope(w_sb, dst_sb, dst_off, oc, wa, par):
                    acc = psB[:, 1024 * par: 1024 * par + 512]
                    swp = psB[:, 1024 * par + 512: 1024 * (par + 1)]
                    evpm = None
                    for kc in range(8):
                        w = (list(wa) + ev_proj_free[par]) if kc == 0 else []
                        evpm = P.op("tensor",
                                    lambda t, kc=kc, oc=oc, w_sb=w_sb, acc=acc: t.matmul(
                                        acc,
                                        w_sb[:, kc * D + 128 * oc: kc * D + 128 * (oc + 1)],
                                        xn2t[:].rearrange("p (tl d) -> p tl d", tl=4)[
                                            :, :, 128 * kc:128 * (kc + 1)],
                                        start=(kc == 0), stop=(kc == 7)),
                                    waits=w, inc=(c_t if kc == 7 else None))
                    evx = P.op("vector",
                               lambda v, acc=acc: v.tensor_copy(ropex, acc),
                               waits=[evpm, ev_ropex_free[0]], inc=c_v)
                    evsw = P.op("tensor",
                                lambda t, swp=swp: t.matmul(swp, swapt[:], ropex,
                                                            start=True, stop=True),
                                waits=[evx, ev_const[0]], inc=c_t)
                    evt1 = P.op("gpsimd",
                                lambda g_, dst_sb=dst_sb, dst_off=dst_off, g=g:
                                g_.tensor_mul(
                                    dst_sb[:, dst_off:dst_off + 512], ropex,
                                    ropec[:, 512 * g:512 * (g + 1)]),
                                waits=[ev_const[2], evx], inc=c_g)
                    evt2 = P.op("vector",
                                lambda v, g=g, swp=swp: v.tensor_mul(
                                    ropet2, swp,
                                    ropes[:, 512 * g:512 * (g + 1)]),
                                waits=[evsw, ev_const[3], ev_ropex_free[1]], inc=c_v)
                    evo = P.op("gpsimd",
                               lambda g_, dst_sb=dst_sb, dst_off=dst_off: g_.tensor_add(
                                   dst_sb[:, dst_off:dst_off + 512],
                                   dst_sb[:, dst_off:dst_off + 512], ropet2),
                               waits=[evt1, evt2], inc=c_g)
                    ev_ropex_free[0] = evt1
                    ev_ropex_free[1] = evo
                    ev_proj_free[par] = [evo]
                    return evo

                ev_qrope = []
                ev_krope = []
                ev_proj_free = {0: ev_psB_free, 1: ev_psB_free}
                ev_ropex_free = [None, None]
                for oc in range(8):
                    wa = ev_xn2t + [ev_wqs, ev_qtb_free]
                    ev_qrope.append(proj_rope(wq_sb, qtb, 512 * oc, oc, wa, 0))
                    wa2 = ev_xn2t + [ev_w[("wk", 7)], ev_qtb_free]
                    ev_krope.append(proj_rope(wk_sb, ktb, SEQ * oc + 512 * g, oc, wa2, 1))
                ev_ktile[(s_i, g)] = ev_krope
                ev_psB_free = ev_proj_free[0] + ev_proj_free[1]

                nk = 4 * (g + 1)
                ev_att_done = [None] * 8
                last_exps = []
                evrl_last = None
                pend_av = None
                pend_epi = None
                for hp in range(8):
                    par = hp % 2
                    if par == 0:
                        avA, avB = psC[0:64, :], psD[64:128, :]
                        smA, smB = psA[0:64, 0:512], psA[64:128, 512:1024]
                    else:
                        avA, avB = psA[0:64, 0:512], psA[64:128, 512:1024]
                        smA, smB = psC[0:64, :], psD[64:128, :]
                    evav = None
                    ev_av4 = None
                    exps = []
                    for j in range(nk):
                        sloc = 1024 * (j % 2)
                        w = []
                        if len(exps) >= 2:
                            w = [exps[-2]]
                        elif last_exps:
                            w = [last_exps[-1]]
                        else:
                            w = list(ev_psB_free)
                        if j == 0 and hp == 0:
                            w += [ev_qrope[7], ev_krope[7]]
                            for gg in range(g):
                                w.append(ev_ktile[(s_i, gg)][hp])
                        if j == 0:
                            w += [ev_qrope[hp], ev_krope[hp]]
                        P.op("tensor",
                             lambda t, j=j, hp=hp, sloc=sloc: t.matmul(
                                 psB[:, sloc: sloc + 512],
                                 ktb[0:64, SEQ * hp + 128 * j: SEQ * hp + 128 * (j + 1)],
                                 qtb[0:64, 512 * hp:512 * (hp + 1)],
                                 start=True, stop=True, tile_position=(0, 0)),
                             waits=w, inc=None)
                        evqk = P.op("tensor",
                                    lambda t, j=j, hp=hp, sloc=sloc: t.matmul(
                                        psB[:, sloc + 512: sloc + 1024],
                                        ktb[64:128,
                                            SEQ * hp + 128 * j: SEQ * hp + 128 * (j + 1)],
                                        qtb[64:128, 512 * hp:512 * (hp + 1)],
                                        start=True, stop=True, tile_position=(64, 0)),
                                    waits=[], inc=c_t)
                        pb = j % 2
                        evex = P.op("scalar",
                                    lambda s, pb=pb, sloc=sloc: s.activation(
                                        pbuf[:, pb * D:(pb + 1) * D],
                                        psB[:, sloc:sloc + 1024], AF.Exp),
                                    waits=[evqk, ev_pbuf_free[pb]], inc=c_s)
                        exps.append(evex)
                        r = j - 4 * g
                        evfix = evex
                        if r >= 0:
                            if r > 0:
                                P.op("vector",
                                     lambda v, pb=pb, r=r: v.memset(
                                         pbuf[:, pb * D: pb * D + 128 * r], 0.0),
                                     waits=[evex], inc=None)
                                P.op("vector",
                                     lambda v, pb=pb, r=r: v.memset(
                                         pbuf[:, pb * D + 512: pb * D + 512 + 128 * r], 0.0),
                                     waits=[evex], inc=None)
                            P.op("vector",
                                 lambda v, pb=pb, r=r: v.tensor_mul(
                                     pbuf[:, pb * D + 128 * r: pb * D + 128 * (r + 1)],
                                     pbuf[:, pb * D + 128 * r: pb * D + 128 * (r + 1)],
                                     trit[:]),
                                 waits=[evex, ev_const[1]], inc=None)
                            evfix = P.op("vector",
                                         lambda v, pb=pb, r=r: v.tensor_mul(
                                             pbuf[:, pb * D + 512 + 128 * r:
                                                  pb * D + 512 + 128 * (r + 1)],
                                             pbuf[:, pb * D + 512 + 128 * r:
                                                  pb * D + 512 + 128 * (r + 1)],
                                             trit[:]),
                                         waits=[evex], inc=c_v)
                        if pend_av is not None:
                            pend_av()
                            pend_av = None
                        if pend_epi is not None:
                            pend_epi()
                            pend_epi = None
                        wav = [evfix, ev_vproj[j], ev_ones]
                        if j == 0:
                            if par == 0:
                                wav += [ev_free_CD, ev_free_A01]
                            else:
                                wav += [ev_free_A01, ev_free_CD]
                        last = (j == nk - 1)

                        def rec_av(j=j, hp=hp, pb=pb, last=last, wav=wav,
                                   avA=avA, avB=avB, smA=smA, smB=smB):
                            nonlocal evav, ev_av4
                            evAVa = P.op("tensor",
                                         lambda t, j=j, hp=hp, pb=pb, last=last, avA=avA:
                                         t.matmul(
                                             avA,
                                             vtb[:, j * D + 128 * hp: j * D + 128 * hp + 64],
                                             pbuf[:, pb * D: pb * D + 512],
                                             start=(j == 0), stop=last,
                                             tile_position=(0, 0)),
                                         waits=wav, inc=(c_t if last else None))
                            evAVb = P.op("tensor",
                                         lambda t, j=j, hp=hp, pb=pb, last=last, avB=avB:
                                         t.matmul(
                                             avB,
                                             vtb[:, j * D + 128 * hp + 64:
                                                 j * D + 128 * (hp + 1)],
                                             pbuf[:, pb * D + 512: pb * D + 1024],
                                             start=(j == 0), stop=last,
                                             tile_position=(0, 64)),
                                         waits=[], inc=(c_t if last else None))
                            evSa = P.op("tensor",
                                        lambda t, j=j, pb=pb, last=last, smA=smA: t.matmul(
                                            smA, ones64[:, 0:64],
                                            pbuf[:, pb * D: pb * D + 512],
                                            start=(j == 0), stop=last,
                                            tile_position=(0, 0)),
                                        waits=[], inc=(c_t if last else None))
                            evav = P.op("tensor",
                                        lambda t, j=j, pb=pb, last=last, smB=smB: t.matmul(
                                            smB, ones64[:, 0:64],
                                            pbuf[:, pb * D + 512: pb * D + 1024],
                                            start=(j == 0), stop=last,
                                            tile_position=(0, 64)),
                                        waits=[], inc=c_t)
                            if last:
                                ev_av4 = [evAVa, evAVb, evSa, evav]
                            ev_pbuf_free[pb] = evav
                        pend_av = rec_av
                    last_exps = exps

                    def rec_epi(hp=hp, par=par, avA=avA, avB=avB, smA=smA, smB=smB):
                        nonlocal ev_free_CD, ev_free_A01, evrl_last
                        evacA = P.op("vector",
                                     lambda v, hp=hp, avA=avA: v.tensor_copy(
                                         att[0:64, 512 * hp:512 * (hp + 1)], avA),
                                     waits=ev_av4, inc=c_v)
                        evacB = P.op("vector",
                                     lambda v, hp=hp, avB=avB: v.tensor_copy(
                                         att[64:128, 512 * hp:512 * (hp + 1)], avB),
                                     waits=[], inc=c_v)
                        evrlA = P.op("scalar",
                                     lambda s, smA=smA, hp=hp: s.activation(
                                         rsc2[0:64, 512 * (hp % 2):512 * (hp % 2) + 512],
                                         smA, AF.Ln),
                                     waits=ev_av4 + [ev_rsc_free[par]], inc=c_s)
                        evrl = P.op("scalar",
                                    lambda s, smB=smB, hp=hp: s.activation(
                                        rsc2[64:128, 512 * (hp % 2):512 * (hp % 2) + 512],
                                        smB, AF.Ln),
                                    waits=[evrlA], inc=c_s)
                        evre = P.op("scalar",
                                    lambda s, hp=hp: s.activation(
                                        rsc2[:, 512 * (hp % 2):512 * (hp % 2) + 512],
                                        rsc2[:, 512 * (hp % 2):512 * (hp % 2) + 512],
                                        AF.Exp, scale=-1.0),
                                    waits=[evrl], inc=c_s)
                        evatA = P.op("vector",
                                     lambda v, hp=hp: v.tensor_mul(
                                         att[0:64, 512 * hp:512 * (hp + 1)],
                                         att[0:64, 512 * hp:512 * (hp + 1)],
                                         rsc2[0:64, 512 * (hp % 2):512 * (hp % 2) + 512]),
                                     waits=[evre], inc=c_v)
                        evat = P.op("vector",
                                    lambda v, hp=hp: v.tensor_mul(
                                        att[64:128, 512 * hp:512 * (hp + 1)],
                                        att[64:128, 512 * hp:512 * (hp + 1)],
                                        rsc2[64:128, 512 * (hp % 2):512 * (hp % 2) + 512]),
                                    waits=[evatA], inc=c_v)
                        ev_rsc_free[par] = evat
                        if par == 0:
                            ev_free_CD = evacB
                            ev_free_A01 = evrl
                        else:
                            ev_free_A01 = evacB
                            ev_free_CD = evrl
                        ev_att_done[hp] = evat
                        evrl_last = evrl
                    pend_epi = rec_epi
                pend_av()
                pend_epi()
                ev_qtb_free = ev_att_done[7]
                ev_psB_free = [last_exps[-1], last_exps[-2] if len(last_exps) > 1 else None]

                if "attn" in taps and s_i == 0:
                    for hp in range(8):
                        for hf in range(2):
                            s_gl.dma("gpsimd",
                                     lambda g_, hp=hp, g2=g, hf=hf: g_.dma_start(
                                         out=taps["attn"][128 * hp + 64 * hf:
                                                          128 * hp + 64 * (hf + 1),
                                                          512 * g2:512 * (g2 + 1)],
                                         in_=att[64 * hf:64 * (hf + 1),
                                                 512 * hp:512 * (hp + 1)]),
                                     waits=[ev_att_done[hp]])

                ev_oproj_free = {0: None, 1: None}
                for tau in range(4):
                    gt = gbase + tau
                    sl = tau
                    if tau % 2 == 0:
                        oA, oB = psA[:, 0:512], psA[:, 512:1024]
                    else:
                        oA, oB = psC[:], psD[:]
                    evom = None
                    for hp in range(8):
                        if hp == 0:
                            if tau == 0:
                                w = ev_att_done + [ev_psA_free, ev_free_A01,
                                                   ev_free_CD, ev_w[("wo", 7)]]
                            elif tau == 1:
                                w = [ev_free_A01, ev_free_CD]
                            else:
                                w = [ev_oproj_free[tau % 2]]
                        else:
                            w = []
                        P.op("tensor",
                             lambda t, hp=hp, tau=tau, oA=oA: t.matmul(
                                 oA,
                                 att[:, 512 * hp + 128 * tau: 512 * hp + 128 * (tau + 1)],
                                 wo_sb[:, hp * D: hp * D + 512],
                                 start=(hp == 0), stop=False),
                             waits=w, inc=None)
                        P.op("tensor",
                             lambda t, hp=hp, tau=tau, oB=oB: t.matmul(
                                 oB,
                                 att[:, 512 * hp + 128 * tau:
                                      512 * hp + 128 * (tau + 1)],
                                 wo_sb[:, hp * D + 512: hp * D + 1024],
                                 start=(hp == 0), stop=False),
                             waits=[], inc=None)
                    P.op("tensor",
                         lambda t, oA=oA: t.matmul(oA, ones1b[0:1, :],
                                                   fbrow[0:1, 1024:1536],
                                                   start=False, stop=True),
                         waits=[ev_const[4], ev_ones1b], inc=None)
                    evom = P.op("tensor",
                                lambda t, oB=oB: t.matmul(oB, ones1b[0:1, :],
                                                          fbrow[0:1, 1536:2048],
                                                          start=False, stop=True),
                                waits=[], inc=c_t)
                    if tau % 2 == 0:
                        eve1 = P.op("scalar",
                                    lambda s: s.copy(epi[:], psA[:]),
                                    waits=[evom], inc=c_s)
                        ev_psA_free = eve1
                    else:
                        P.op("scalar",
                             lambda s: s.copy(epi[:, 0:512], psC[:]),
                             waits=[evom], inc=c_s)
                        eve1 = P.op("scalar",
                                    lambda s: s.copy(epi[:, 512:1024], psD[:]),
                                    waits=[], inc=c_s)
                    ev_oproj_free[tau % 2] = eve1
                    evx2 = P.op("gpsimd",
                                lambda g_, sl=sl: g_.tensor_add(
                                    xbuf[:, sl * D:(sl + 1) * D], epi[:],
                                    xbuf[:, sl * D:(sl + 1) * D]),
                                waits=[eve1, ev_tapx1[sl]], inc=c_g)
                    evid3 = P.op("vector",
                                 lambda v, sl=sl, gt=gt: v.reduce_sum(
                                     st3[:, gt:gt + 1],
                                     xbuf[:, sl * D:(sl + 1) * D], axis=AX),
                                 waits=[evx2], inc=c_v)
                    evs3 = P.op("scalar",
                                lambda s, sl=sl, gt=gt: s.activation(
                                    psB[:, 0:1024], xbuf[:, sl * D:(sl + 1) * D],
                                    AF.Square, accum_out=sq3[:, gt:gt + 1]),
                                waits=[evx2, ev_acc_prev], inc=c_s)
                    ev_acc_prev = evs3
                    P.op("vector",
                         lambda v, sl=sl, gt=gt: v.reduce_max(
                             rmx3[:, gt:gt + 1], xbuf[:, sl * D:(sl + 1) * D], axis=AX),
                         waits=[evx2], inc=None)
                    evv3b = P.op("vector",
                                 lambda v, sl=sl, gt=gt: v.tensor_reduce(
                                     rmn3[:, gt:gt + 1], xbuf[:, sl * D:(sl + 1) * D],
                                     AX, ALU.min),
                                 waits=[evx2], inc=c_v)
                    evst = s_st.dma("scalar",
                                    lambda s_, gt=gt, sl=sl: s_.dma_start(
                                        out=x2_h[128 * gt:128 * (gt + 1), :],
                                        in_=xbuf[:, sl * D:(sl + 1) * D]),
                                    waits=[evs3, evv3b, evid3])
                    if "x2" in taps:
                        evst = S_GP.dma("gpsimd",
                                        lambda g_, gt=gt, sl=sl: g_.dma_start(
                                            out=taps["x2"][128 * gt:128 * (gt + 1), :],
                                            in_=xbuf[:, sl * D:(sl + 1) * D]),
                                        waits=[evst])
                    ev_xfree[sl] = evst
                ev_free_A01 = ev_oproj_free[0]
                ev_free_CD = ev_oproj_free[1]

        if "kt" in taps:
            for hp in range(8):
                S_GP.dma("gpsimd",
                      lambda g_, hp=hp: g_.dma_start(
                          out=taps["kt"][128 * hp:128 * (hp + 1), :],
                          in_=ktb[:, SEQ * hp:SEQ * (hp + 1)]),
                      waits=[ev_ktile[(NSEQ - 1, NG - 1)][hp]])
        if "qt" in taps:
            for hp in range(8):
                S_GP.dma("gpsimd",
                      lambda g_, hp=hp: g_.dma_start(
                          out=taps["qt"][128 * hp:128 * (hp + 1), 1536:2048],
                          in_=qtb[:, 512 * hp:512 * (hp + 1)]),
                      waits=[ev_qrope[hp]])

        ev_allst = s_st.all_done()
        psT_free["p4o"] = ev_free_CD
        evm3 = P.op("vector", lambda v: v.tensor_scalar(m3[:], st3[:], 1.0 / D, None,
                                                        ALU.mult),
                    waits=list(ev_xfree), inc=c_v)
        P.op("vector", lambda v: v.tensor_scalar(sq3[:], sq3[:], 1.0 / D, None, ALU.mult),
             inc=None)
        P.op("vector", lambda v: v.scalar_tensor_tensor(st3[:], m3[:], -1.0, m3[:],
                                                        ALU.mult, ALU.mult), inc=None)
        P.op("vector", lambda v: v.tensor_add(st3[:], sq3[:], st3[:]), inc=None)
        evv3c = P.op("vector", lambda v: v.tensor_scalar(st3[:], st3[:], EPS, None, ALU.add),
                     inc=c_v)
        evln3 = P.op("scalar", lambda s: s.activation(sq3[:], st3[:], AF.Ln),
                     waits=[evv3c], inc=c_s)
        evr3 = P.op("scalar", lambda s: s.activation(sq3[:], sq3[:], AF.Exp, scale=-0.5),
                    waits=[evln3], inc=c_s)
        P.op("vector", lambda v: v.tensor_sub(rmx3[:], rmx3[:], m3[:]), waits=[evm3],
             inc=None)
        P.op("vector", lambda v: v.tensor_sub(rmn3[:], m3[:], rmn3[:]), inc=None)
        P.op("vector", lambda v: v.tensor_max(rmx3[:], rmx3[:], rmn3[:]), inc=None)
        P.op("vector", lambda v: v.tensor_mul(rmx3[:], rmx3[:], sq3[:]), waits=[evr3],
             inc=None)
        eva5b = P.op("vector", lambda v: v.reduce_max(sc[:, 2:3], rmx3[:], axis=AX), inc=c_v)
        evgm2 = pmax_flip(sc[:, 2:3], scr[0:1, 8:9], [eva5b])
        evrep2 = P.op("vector",
                      lambda v: v.tensor_scalar(flip_row[0:1, 0:8], onesrow[0:1, 0:8],
                                                scr[0:1, 8:9], None, ALU.mult),
                      waits=[evgm2], inc=c_v)
        evci2 = S_GP.dma("gpsimd", lambda g: g.dma_start(out=cc2i.ap(), in_=flip_row[0:1, 0:8]),
                      waits=[evrep2])
        evcc2 = P.op("gpsimd",
                     lambda g: g.collective_compute(
                         "AllReduce", ALU.max, replica_groups=[list(range(NCORE))],
                         ins=[cc2i.ap()], outs=[cc2o.ap()]),
                     waits=[evci2, evcc], inc=c_cc)
        evco2 = S_GP.dma("gpsimd", lambda g: g.dma_start(out=flip_row[0:1, 0:8], in_=cc2o.ap()),
                      waits=[evcc2])
        evis2 = P.op("vector", lambda v: v.tensor_scalar(scr[0:1, 9:10], flip_row[0:1, 0:1],
                                                         1.0 / 127.0, None, ALU.mult),
                     waits=[evco2], inc=c_v)
        evrs2 = P.op("vector", lambda v: v.reciprocal(scr[0:1, 10:11], scr[0:1, 9:10]),
                     waits=[evis2], inc=c_v)
        ev_rsb2 = bcast128(rs2128[:, 0:1], scr[0:1, 10:11], [evrs2])
        evch_ = P.op("vector", lambda v: v.tensor_mul(scr[0:1, 11:12], scr[0:1, 0:1],
                                                      scr[0:1, 9:10]),
                     waits=[evis2], inc=c_v)
        evch2_ = P.op("vector", lambda v: v.tensor_scalar(scr[0:1, 11:12], scr[0:1, 11:12],
                                                          0.5, None, ALU.mult),
                      waits=[evch_], inc=c_v)
        ev_chb2 = bcast128(ch2128[:, 0:1], scr[0:1, 11:12], [evch2_])
        evfb2a = P.op("vector", lambda v: v.reciprocal(scr[0:1, 13:14], scr[0:1, 11:12]),
                      waits=[evch2_], inc=c_v)
        P.op("vector", lambda v: v.tensor_mul(scr[0:1, 13:14], scr[0:1, 13:14],
                                              scr[0:1, 5:6]), inc=None)
        ev_fb2 = P.op("vector",
                      lambda v: v.tensor_single_scalar(fbrow[0:1, 1024:2048],
                                                       fbrow[0:1, 0:1024],
                                                       scr[0:1, 13:14], ALU.mult),
                      waits=[evfb2a, (c_t, c_t.n)], inc=c_v)
        evr3s = P.op("vector",
                     lambda v: v.tensor_single_scalar(r3s[:], sq3[:], rs2128[:, 0:1],
                                                      ALU.mult),
                     waits=[ev_rsb2, evr3], inc=c_v)
        evb3s = P.op("vector",
                     lambda v: v.scalar_tensor_tensor(b3s[:], m3[:], -1.0, r3s[:],
                                                      ALU.mult, ALU.mult),
                     waits=[evr3s], inc=c_v)
        evb3s = P.op("vector",
                     lambda v: v.tensor_single_scalar(b3s[:], b3s[:], 256.5, ALU.add),
                     inc=c_v)

        ev_epi_free = {0: None, 1: None}
        ev_acc_free = {0: [ev_psA_free], 1: None}
        ev_p4free = [evb3s, evb3s, evb3s, evb3s]
        ev_last_out = None

        def p4_load(gt):
            sl = gt % 4
            e1 = s_ld.dma("sync",
                          lambda s, gt=gt, sl=sl: s.dma_start(
                              out=xbuf[:, sl * D:(sl + 1) * D],
                              in_=x2_h[128 * gt:128 * (gt + 1), :]),
                          waits=[ev_p4free[sl], ev_allst])
            return [e1]

        evls = {0: p4_load(0), 1: p4_load(1), 2: p4_load(2)}
        ev_q = quant_tile(0, r3s[:, 0:1], b3s[:, 0:1], evls[0])
        for gt in range(NT):
            sl = gt % 4
            par = gt % 2
            if par == 0:
                acc = (psA[:, 0:512], psA[:, 512:1024])
                tkey, tregions = "p2", None
            else:
                acc = (psB[:, 0:512], psB[:, 512:1024])
                tkey, tregions = "p4o", (psC, psD)
            evmm = ffn_mm(ev_q, 1024, ev_fb2, acc=acc,
                          acc_wait=[ev_acc_free[par]], tkey=tkey, tregions=tregions)
            if gt + 3 < NT:
                evls[gt + 3] = p4_load(gt + 3)
            if gt < NT - 1:
                ev_q = quant_tile((gt + 1) % 4, r3s[:, gt + 1:gt + 2],
                                  b3s[:, gt + 1:gt + 2], evls[gt + 1])
            ebuf = epi if par == 0 else rsc2
            if par == 0:
                eve1 = P.op("scalar",
                            lambda s: s.activation(epi[:], psA[:], AF.Identity,
                                                   scale=ch2128[:, 0:1]),
                            waits=[evmm, ev_chb2, ev_epi_free[0]], inc=c_s)
            else:
                P.op("scalar",
                     lambda s: s.activation(rsc2[:, 0:512], psB[:, 0:512],
                                            AF.Identity, scale=ch2128[:, 0:1]),
                     waits=[evmm, ev_chb2, ev_epi_free[1]], inc=c_s)
                eve1 = P.op("scalar",
                            lambda s: s.activation(rsc2[:, 512:1024],
                                                   psB[:, 512:1024],
                                                   AF.Identity, scale=ch2128[:, 0:1]),
                            waits=[], inc=c_s)
            ev_acc_free[par] = eve1
            evo = P.op("gpsimd",
                       lambda g_, sl=sl, ebuf=ebuf: g_.tensor_add(
                           ebuf[:, 0:1024], ebuf[:, 0:1024],
                           xbuf[:, sl * D:(sl + 1) * D]),
                       waits=[eve1], inc=c_g)
            evs1 = s_st.dma("scalar",
                            lambda s_, gt=gt, ebuf=ebuf: s_.dma_start(
                                out=out_h[128 * gt:128 * (gt + 1), :],
                                in_=ebuf[:, 0:1024]),
                            waits=[evo])
            ev_epi_free[par] = evs1
            ev_p4free[sl] = evo
            ev_last_out = evs1

        P.op("gpsimd", lambda g: g.nop(), waits=[ev_last_out], inc=None)

        P.emit(block)
    return nc


def _host_constants():
    HD = 64
    inv = 1.0 / (10000.0 ** (np.arange(0, HD, 2)[: HD // 2].astype(np.float64) / HD))
    ang = np.outer(np.arange(SEQ, dtype=np.float64), inv)
    cos = np.cos(ang).astype(np.float32)
    sin = np.sin(ang).astype(np.float32)
    ropec = np.concatenate([cos.T] * 4, axis=0).astype(ml_dtypes.bfloat16)
    ropes = np.concatenate([-sin.T, sin.T, -sin.T, sin.T], axis=0).astype(ml_dtypes.bfloat16)
    ident = np.eye(128, dtype=ml_dtypes.bfloat16)
    permsrc = np.concatenate([np.arange(32, 64), np.arange(0, 32),
                              np.arange(96, 128), np.arange(64, 96)])
    swap = np.zeros((128, 128), dtype=ml_dtypes.bfloat16)
    swap[permsrc, np.arange(128)] = 1
    tri = np.triu(np.ones((128, 128), dtype=np.float32)).astype(ml_dtypes.bfloat16)
    perm = np.zeros(D, dtype=np.int64)
    for c in range(8):
        base = 128 * c
        hA, hB = 2 * c, 2 * c + 1
        perm[base + 0:base + 32] = 64 * hA + 2 * np.arange(32)
        perm[base + 32:base + 64] = 64 * hA + 2 * np.arange(32) + 1
        perm[base + 64:base + 96] = 64 * hB + 2 * np.arange(32)
        perm[base + 96:base + 128] = 64 * hB + 2 * np.arange(32) + 1
    return ropec, ropes, ident, swap, tri, perm


_NC_CACHE = {}


def kernel(L0_attn_x, L0_attn_wq, L0_attn_wk, L0_attn_wv, L0_attn_wo_w,
           L0_attn_wo_b, L0_ffn_weight, L0_ffn_bias, _debug_taps=()):
    x = np.ascontiguousarray(np.asarray(L0_attn_x, dtype=np.float32))
    wq = np.asarray(L0_attn_wq, np.float32)
    wk = np.asarray(L0_attn_wk, np.float32)
    wv = np.ascontiguousarray(np.asarray(L0_attn_wv, np.float32))
    wo = np.ascontiguousarray(np.asarray(L0_attn_wo_w, np.float32))
    wob = np.asarray(L0_attn_wo_b, np.float32)
    fw = np.ascontiguousarray(np.asarray(L0_ffn_weight, np.float32))
    ffb = np.asarray(L0_ffn_bias, np.float32)

    ropec, ropes, ident, swap, tri, perm = _host_constants()
    wq_p = np.ascontiguousarray(wq[:, perm])
    wk_p = np.ascontiguousarray(wk[:, perm])
    wob_t = np.ascontiguousarray(
        np.broadcast_to(wob[None, :], (128, D))).astype(ml_dtypes.bfloat16)
    ffb_t = np.ascontiguousarray(
        np.broadcast_to(ffb[None, :], (128, D))).astype(ml_dtypes.bfloat16)

    key = tuple(_debug_taps)
    if key not in _NC_CACHE:
        _NC_CACHE[key] = build_program(_debug_taps)
    nc = _NC_CACHE[key]

    in_maps = []
    for c in range(NCORE):
        xs = np.ascontiguousarray(x[4 * c:4 * c + 4].reshape(ROWS, D))
        in_maps.append({
            "x_in": xs, "wq_h": wq_p, "wk_h": wk_p, "wv_h": wv, "wo_h": wo,
            "fw_h": fw, "wob_h": wob_t, "ffb_h": ffb_t,
            "ident_h": ident, "identf_h": ident.astype(np.float32),
            "swap_h": swap, "tri_h": tri,
            "ropec_h": ropec, "ropes_h": ropes,
        })
    res = run_bass_kernel_spmd(nc, in_maps, core_ids=list(range(NCORE)))
    out = np.stack([np.asarray(r["out_h"]).reshape(NSEQ, SEQ, D) for r in res.results])
    out = np.ascontiguousarray(out.reshape(32, SEQ, D))
    if _debug_taps:
        return out, res
    return out


# version 22 (final)
# speedup vs baseline: 1.0579x; 1.0579x over previous
import numpy as np
import ml_dtypes
import concourse.bass as bass
import concourse.mybir as mybir
from concourse.bass_utils import run_bass_kernel_spmd
from contextlib import ExitStack

F32 = mybir.dt.float32
BF16 = mybir.dt.bfloat16
I32 = mybir.dt.int32
AF = mybir.ActivationFunctionType
ALU = mybir.AluOpType
AX = mybir.AxisListType.X

NCORE = 8
SEQ = 2048
D = 1024
NSEQ = 4
ROWS = NSEQ * SEQ
NT = ROWS // 128
NG = 4
EPS = 1e-5


class Ctr:
    __slots__ = ("name", "h", "n")

    def __init__(self, name):
        self.name = name
        self.h = None
        self.n = 0


class Prog:
    ENGS = ("sync", "tensor", "vector", "scalar", "gpsimd")

    def __init__(self):
        self.ops = {e: [] for e in self.ENGS}
        self.ctrs = []

    def ctr(self, name):
        c = Ctr(name)
        self.ctrs.append(c)
        return c

    CHAINED = ("vector", "scalar", "gpsimd")

    def op(self, eng, fn, waits=(), inc=None, inc_n=1):
        w = []
        for x in waits:
            if x is None:
                continue
            if isinstance(x, list):
                w += [y for y in x if y is not None]
            else:
                w.append(x)
        if eng in self.CHAINED:
            c = self.chain[eng]
            if c.n > 0:
                w.append((c, c.n))
            if inc_n == 16:
                pass
            elif inc is None:
                inc = c
            elif inc is not c:
                inc = c
        ev = None
        if inc is not None:
            inc.n += inc_n
            ev = (inc, inc.n)
        self.ops[eng].append((fn, w, inc, inc_n))
        return ev

    def dma(self, eng, fn, waits=(), inc=None):
        return self.op(eng, fn, waits, inc, 16)

    def stream(self, name, k=4):
        return DmaStream(self, name, k)

    def emit(self, block):
        def mk(eng_name):
            def run(eng):
                hi = {}
                for fn, waits, inc, inc_n in self.ops[eng_name]:
                    for c, v in waits:
                        if hi.get(c.name, 0) >= v:
                            continue
                        eng.wait_ge(c.h, v)
                        hi[c.name] = v
                    inst = fn(eng)
                    if inc is not None:
                        inst.then_inc(inc.h, inc_n)
            return run

        for name in self.ENGS:
            if self.ops[name]:
                getattr(block, name)(mk(name))


class DmaStream:

    def __init__(self, P, name, k):
        self.P = P
        self.ctrs = [P.ctr(f"{name}{j}") for j in range(k)]
        self.k = k
        self.idx = 0

    def dma(self, eng, fn, waits=()):
        j = self.idx % self.k
        rnd = self.idx // self.k
        self.idx += 1
        c = self.ctrs[j]
        w = list(waits) + ([(c, 16 * rnd)] if rnd > 0 else [])
        self.P.op(eng, fn, w, c, 16)
        return (c, 16 * (rnd + 1))

    def all_done(self):
        evs = []
        for j, c in enumerate(self.ctrs):
            n = (self.idx - j + self.k - 1) // self.k
            if n > 0:
                evs.append((c, 16 * n))
        return evs


def build_program(debug_taps=()):
    nc = bass.Bass("TRN2", target_bir_lowering=False, dynamic_dma_scratch_size=128)

    x_in = nc.dram_tensor("x_in", [ROWS, D], F32, kind="ExternalInput").ap()
    wq_h = nc.dram_tensor("wq_h", [D, D], F32, kind="ExternalInput").ap()
    wk_h = nc.dram_tensor("wk_h", [D, D], F32, kind="ExternalInput").ap()
    wv_h = nc.dram_tensor("wv_h", [D, D], F32, kind="ExternalInput").ap()
    wo_h = nc.dram_tensor("wo_h", [D, D], F32, kind="ExternalInput").ap()
    fw_h = nc.dram_tensor("fw_h", [D, D], F32, kind="ExternalInput").ap()
    wob_h = nc.dram_tensor("wob_h", [128, D], BF16, kind="ExternalInput").ap()
    ffb_h = nc.dram_tensor("ffb_h", [128, D], BF16, kind="ExternalInput").ap()
    ident_h = nc.dram_tensor("ident_h", [128, 128], BF16, kind="ExternalInput").ap()
    identf_h = nc.dram_tensor("identf_h", [128, 128], F32, kind="ExternalInput").ap()
    swap_h = nc.dram_tensor("swap_h", [128, 128], BF16, kind="ExternalInput").ap()
    tri_h = nc.dram_tensor("tri_h", [128, 128], BF16, kind="ExternalInput").ap()
    ropec_h = nc.dram_tensor("ropec_h", [128, SEQ], BF16, kind="ExternalInput").ap()
    ropes_h = nc.dram_tensor("ropes_h", [128, SEQ], BF16, kind="ExternalInput").ap()
    out_h = nc.dram_tensor("out_h", [ROWS, D], F32, kind="ExternalOutput").ap()

    x2_h = nc.dram_tensor("x2_h", [ROWS, D], F32).ap()
    flip_h = nc.dram_tensor("flip_h", [128], F32).ap()
    flip_pc = flip_h.rearrange("(p o) -> p o", o=1)
    flip_rc = flip_h.rearrange("(o p) -> o p", o=1)
    cc1i = nc.dram_tensor("cc1i", [1, 8], F32)
    cc1o = nc.dram_tensor("cc1o", [1, 8], F32)
    cc2i = nc.dram_tensor("cc2i", [1, 8], F32)
    cc2o = nc.dram_tensor("cc2o", [1, 8], F32)

    taps = {}
    for t in debug_taps:
        if t in ("x1", "x2", "qin1"):
            taps[t] = nc.dram_tensor("tap_" + t, [ROWS, D], F32, kind="ExternalOutput").ap()
        elif t in ("kt", "qt"):
            taps[t] = nc.dram_tensor("tap_" + t, [D, SEQ], F32, kind="ExternalOutput").ap()
        elif t == "attn":
            taps[t] = nc.dram_tensor("tap_attn", [D, SEQ], F32, kind="ExternalOutput").ap()

    P = Prog()
    s_ld = P.stream("ld", 4)
    s_ld2 = P.stream("ld2", 4)
    s_gl = P.stream("gl", 4)
    s_st = P.stream("st", 4)
    s_st2 = P.stream("st2", 4)
    s_wt = P.stream("wt", 4)

    class S_LD:
        @staticmethod
        def dma(eng, fn, waits=()):
            return s_ld.dma(eng, fn, waits)

    class S_GP:
        @staticmethod
        def dma(eng, fn, waits=(), _st=False):
            if _st:
                return s_st.dma("sync", fn, waits)
            return s_gl.dma("sync", fn, waits)
    c_t = P.ctr("t")
    c_v = P.ctr("v")
    c_s = P.ctr("s")
    c_cc = P.ctr("cc")
    c_g = P.ctr("g")
    P.chain = {"vector": c_v, "scalar": c_s, "gpsimd": c_g}

    with ExitStack() as ctx:
        e = ctx.enter_context

        def sb(name, shape, dt):
            return e(nc.sbuf_tensor(name, shape, dt))

        wq_sb = sb("wq_sb", [128, 8 * D], BF16)
        wk_sb = sb("wk_sb", [128, 8 * D], BF16)
        wv_sb = sb("wv_sb", [128, 8 * D], BF16)
        wo_sb = sb("wo_sb", [128, 8 * D], BF16)
        qwt_sb = sb("qwt_sb", [128, 8 * D], BF16)
        ktb = sb("ktb", [128, 8 * SEQ], BF16)
        vtb = sb("vtb", [128, 16 * D], BF16)
        qtb = sb("qtb", [128, 8 * 512], BF16)
        xn2t = sb("xn2t", [128, 4 * D], BF16)
        xbuf = sb("xbuf", [128, 4 * D], F32)
        pbuf = sb("pbuf", [128, 2 * D], BF16)
        att = sb("att", [128, 8 * 512], BF16)
        qint = att
        ropex = pbuf[:, 0:512]
        ropet2 = pbuf[:, 512:1024]
        tq = sb("tq", [128, D], F32)
        q32 = sb("q32", [128, D // 2], I32)
        qout = sb("qout", [128, D // 2], F32)
        epi = sb("epi", [128, D], F32)
        rsc2 = sb("rsc2", [128, D], F32)

        identf = sb("identf", [128, 128], F32)
        swapt = sb("swapt", [128, 128], BF16)
        trit = sb("trit", [128, 128], BF16)
        ropec = sb("ropec", [128, SEQ], BF16)
        ropes = sb("ropes", [128, SEQ], BF16)
        ones64 = sb("ones64", [128, 64], BF16)
        onesrow = sb("onesrow", [1, 128], F32)
        onescol = sb("onescol", [128, 1], F32)
        ones1b = sb("ones1b", [1, 128], BF16)
        fbrow = sb("fbrow", [1, 2048], BF16)
        flip_row = sb("flip_row", [1, 128], F32)
        st1 = sb("st1", [128, NT], F32)
        sq1 = sb("sq1", [128, NT], F32)
        rmx = sb("rmx", [128, NT], F32)
        rmn = sb("rmn", [128, NT], F32)
        m1 = sb("m1", [128, NT], F32)
        r1s = sb("r1s", [128, NT], F32)
        b1s = sb("b1s", [128, NT], F32)
        st3, sq3, rmx3, rmn3, m3 = st1, sq1, rmx, rmn, m1
        r3s = sb("r3s", [128, NT], F32)
        b3s = sb("b3s", [128, NT], F32)
        sc = sb("sc", [128, 8], F32)
        scr = sb("scr", [1, 16], F32)
        wsum = sb("wsum", [128, 8], F32)
        m2t = sb("m2t", [128, 4], F32)
        r2t = sb("r2t", [128, 4], F32)
        b2t = sb("b2t", [128, 4], F32)
        v2t = sb("v2t", [128, 4], F32)
        colscr = sb("colscr", [128, 8], F32)
        rs128 = colscr[:, 0:1]
        ch128 = colscr[:, 1:2]
        rs2128 = colscr[:, 2:3]
        ch2128 = colscr[:, 3:4]
        thr128 = colscr[:, 4:5]

        psA = e(nc.psum_tensor("psA", [128, 1024], F32))
        psB = e(nc.psum_tensor("psB", [128, 2048], F32))
        psC = e(nc.psum_tensor("psC", [128, 512], F32))
        psD = e(nc.psum_tensor("psD", [128, 512], F32))
        psT = psB[:, 1024:2048]

        for c in P.ctrs:
            c.h = e(nc.semaphore(c.name))
        block = e(nc.Block())

        ev_flip_free = None

        def bcast128(dst_col, src11, waits):
            nonlocal ev_flip_free
            ev1 = P.op("vector",
                       lambda v, s=src11: v.tensor_scalar(
                           flip_row[0:1, :], onesrow[0:1, :], s, None, ALU.mult),
                       waits=list(waits) + [ev_flip_free], inc=c_v)
            ev2 = S_GP.dma("gpsimd", lambda g: g.dma_start(out=flip_rc, in_=flip_row[0:1, :]),
                        waits=[ev1])
            ev3 = S_GP.dma("gpsimd", lambda g, d=dst_col: g.dma_start(out=d, in_=flip_pc),
                        waits=[ev2])
            ev_flip_free = ev3
            return ev3

        def pmax_flip(src_col, dst11, waits):
            nonlocal ev_flip_free
            ev1 = S_GP.dma("gpsimd", lambda g, s=src_col: g.dma_start(out=flip_pc, in_=s),
                        waits=list(waits) + [ev_flip_free])
            ev2 = S_GP.dma("gpsimd", lambda g: g.dma_start(out=flip_row[0:1, :], in_=flip_rc),
                        waits=[ev1])
            ev3 = P.op("vector",
                       lambda v, d=dst11: v.reduce_max(d, flip_row[0:1, :], axis=AX),
                       waits=[ev2], inc=c_v)
            ev_flip_free = ev3
            return ev3

        psT_free = {"p2": None, "p4o": None}

        def transpose_tile(src_ap, dst_sb, dst_off, wa, identt=None, ev_idw=None,
                           key="p2", regions=None):
            idt = identf if identt is None else identt
            r = regions if regions is not None else (psT[:, 0:512], psT[:, 512:1024])
            evt = None
            for kc in range(8):
                w = list(wa) if kc == 0 else []
                if kc == 0:
                    w.append(psT_free[key])
                    w.append(ev_idw)
                rr = r[kc // 4]
                co = 128 * (kc % 4)
                evt = P.op("tensor",
                           lambda t, kc=kc, rr=rr, co=co, idt=idt: t.transpose(
                               rr[:, co:co + 128],
                               src_ap[:, 128 * kc:128 * (kc + 1)], idt[:]),
                           waits=w, inc=(c_t if kc == 7 else None))
            evc1 = P.op("vector",
                        lambda v, dst_sb=dst_sb, dst_off=dst_off, r=r: v.tensor_copy(
                            dst_sb[:, dst_off:dst_off + 512], r[0][:, 0:512]),
                        waits=[evt], inc=c_v)
            evc2 = P.op("scalar",
                        lambda sc_, dst_sb=dst_sb, dst_off=dst_off, r=r: sc_.copy(
                            dst_sb[:, dst_off + 512:dst_off + D], r[1][:, 0:512]),
                        waits=[evt], inc=c_s)
            psT_free[key] = [evc1, evc2]
            return [evc1, evc2]

        ev_qinb_free = None
        ev_tq_free = [None, None]

        def quant_tile(sl, rS, bS, wa, src=None):
            nonlocal ev_qinb_free
            H2 = D // 2
            evg = [None, None]
            for hh in range(2):
                ho = hh * H2
                sap = xbuf[:, sl * D + ho:sl * D + ho + H2] if src is None \
                    else src[:, ho:ho + H2]
                eva = P.op("scalar",
                           lambda s, sap=sap, rS=rS, bS=bS, ho=ho: s.activation(
                               tq[:, ho:ho + H2], sap, AF.Identity,
                               scale=rS, bias=bS),
                           waits=list(wa) + [ev_qinb_free, ev_tq_free[hh]], inc=c_s)
                P.op("vector",
                     lambda v, ho=ho: v.tensor_copy(q32[:], tq[:, ho:ho + H2]),
                     waits=[eva], inc=None)
                P.op("vector",
                     lambda v: v.tensor_copy(qout[:, 0:H2], q32[:]),
                     inc=None)
                P.op("vector",
                     lambda v, ho=ho: v.tensor_tensor(
                         tq[:, ho:ho + H2], tq[:, ho:ho + H2],
                         qout[:, 0:H2], ALU.is_ge),
                     inc=None)
                ev4 = P.op("vector",
                           lambda v, ho=ho: v.scalar_tensor_tensor(
                               tq[:, ho:ho + H2], tq[:, ho:ho + H2], -257.0,
                               qout[:, 0:H2], ALU.add, ALU.add),
                           inc=c_v)
                ev_tq_free[hh] = ev4
                evg[hh] = ev4
            return [evg[0], evg[1]]

        ev_const = []
        for dst, src in ((swapt, swap_h), (trit, tri_h),
                         (ropec, ropec_h), (ropes, ropes_h),
                         (None, wob_h), (None, ffb_h), (identf, identf_h)):
            if dst is None:
                off = 1024 if src is wob_h else 0
                ev_const.append(S_LD.dma("sync",
                                      lambda s, r=src, off=off: s.dma_start(
                                          out=fbrow[0:1, off:off + 1024],
                                          in_=r[0:1, :])))
            else:
                ev_const.append(S_LD.dma("sync",
                                      lambda s, d=dst, r=src: s.dma_start(out=d[:], in_=r)))
        ev_idf = ev_const[6]
        ev_ones = P.op("vector", lambda v: v.memset(ones64[:], 1.0), inc=c_v)
        ev_ones1b = P.op("vector", lambda v: v.memset(ones1b[:], 1.0), inc=c_v)
        P.op("vector", lambda v: v.memset(onesrow[:], 1.0), inc=c_v)
        ev_onescol = P.op("vector", lambda v: v.memset(onescol[:], 1.0), inc=c_v)
        ev_prev_acc = None
        ev_fw1 = []
        for i in range(8):
            evl = S_LD.dma("scalar",
                        lambda s, i=i: s.dma_start(out=tq[:],
                                                   in_=fw_h[128 * i:128 * (i + 1), :]),
                        waits=[ev_prev_acc])
            eva = P.op("scalar",
                       lambda s, i=i: s.activation(epi[:], tq[:], AF.Abs,
                                                   accum_out=wsum[:, i:i + 1]),
                       waits=[evl], inc=c_s)
            ev_prev_acc = eva
            ev_fw1.append(eva)
        ev_ws0 = P.op("vector", lambda v: v.reduce_sum(sc[:, 0:1], wsum[:], axis=AX),
                      waits=ev_fw1, inc=c_v)
        ev_ws1 = P.op("tensor",
                      lambda t: t.matmul(psB[0:1, 0:1], sc[:, 0:1], onescol[:],
                                         start=True, stop=True),
                      waits=[ev_ws0, ev_onescol], inc=c_t)
        ev_ws2 = P.op("vector",
                      lambda v: v.tensor_scalar(scr[0:1, 0:1], psB[0:1, 0:1],
                                                1.0 / float(D * D), None, ALU.mult),
                      waits=[ev_ws1], inc=c_v)
        ev_thr = P.op("vector",
                      lambda v: v.tensor_scalar(scr[0:1, 1:2], scr[0:1, 0:1],
                                                0.5, None, ALU.mult),
                      waits=[ev_ws2], inc=c_v)
        ev_thrb = bcast128(thr128[:, 0:1], scr[0:1, 1:2], [ev_thr])

        ev_prev = None
        ev_prevq = None
        ev_prevt = None
        for i in range(8):
            evl = S_LD.dma("scalar",
                        lambda s, i=i: s.dma_start(out=tq[:],
                                                   in_=fw_h[128 * i:128 * (i + 1), :]),
                        waits=[ev_fw1[-1] if i == 0 else ev_prev])
            eva = P.op("scalar", lambda s: s.activation(epi[:], tq[:], AF.Abs),
                       waits=[evl, ev_thrb, ev_prevq], inc=c_s)
            evm = P.op("vector",
                       lambda v: v.tensor_single_scalar(epi[:], epi[:], thr128[:, 0:1],
                                                        ALU.is_gt),
                       waits=[eva], inc=c_v)
            evsg = P.op("scalar", lambda s: s.activation(pbuf[:, 0:D], tq[:], AF.Sign),
                        waits=[evl, ev_prevt], inc=c_s)
            ev_prev = evsg
            evq = P.op("vector",
                       lambda v: v.tensor_mul(xbuf[:, 2 * D:3 * D], epi[:], pbuf[:, 0:D]),
                       waits=[evm, evsg], inc=c_v)
            ev_prevq = evq
            evt = None
            for kc in range(8):
                w = [evq, ev_idf] if kc == 0 else []
                if kc == 0:
                    w.append(psT_free["p2"])
                evt = P.op("tensor",
                           lambda t, kc=kc: t.transpose(
                               psT[:, 128 * kc:128 * (kc + 1)],
                               xbuf[:, 2 * D + 128 * kc:2 * D + 128 * (kc + 1)],
                               identf[:]),
                           waits=w, inc=(c_t if kc == 7 else None))
            ev_prevt = evt
            evac = None
            for kc in range(8):
                evac = P.op("vector",
                            lambda v, kc=kc, i=i: v.tensor_copy(
                                qwt_sb[:, kc * D + 128 * i: kc * D + 128 * (i + 1)],
                                psT[:, 128 * kc:128 * (kc + 1)]),
                            waits=[evt] if kc == 0 else [], inc=(c_v if kc == 7 else None))
            psT_free["p2"] = evac

        ev_qinb_free = psT_free["p2"]

        ev_w = {}
        ev_wstage = {2: ev_qinb_free, 3: ev_qinb_free}
        for nm, dst, src in (("wv", wv_sb, wv_h), ("wq", wq_sb, wq_h),
                             ("wk", wk_sb, wk_h), ("wo", wo_sb, wo_h)):
            ceng = "vector" if nm in ("wv", "wq") else "gpsimd"
            for i in range(8):
                wsl = 2 + (i % 2)
                qeng = "sync" if i % 2 == 0 else "scalar"
                evwl = s_wt.dma(
                    qeng,
                    lambda g, r=src, i=i, wsl=wsl: g.dma_start(
                        out=xbuf[:, wsl * D:(wsl + 1) * D],
                        in_=r[128 * i:128 * (i + 1), :]),
                    waits=[ev_wstage[wsl]])
                evc = P.op(ceng,
                           lambda g_, d=dst, i=i, wsl=wsl: g_.tensor_copy(
                               d[:, i * D:(i + 1) * D], xbuf[:, wsl * D:(wsl + 1) * D]),
                           waits=[evwl], inc=(c_v if ceng == "vector" else c_g))
                ev_wstage[wsl] = evc
                ev_w[(nm, i)] = evc

        ev_use = [None, None]
        evs = evv = None

        for i in range(NT):
            sl = i % 2
            evl = s_ld.dma("sync",
                        lambda s, i=i, sl=sl: s.dma_start(
                            out=xbuf[:, sl * D:(sl + 1) * D],
                            in_=x_in[128 * i:128 * (i + 1), :]),
                        waits=[ev_use[sl]])
            evid = P.op("scalar",
                        lambda s, i=i, sl=sl: s.activation(
                            psA[:], xbuf[:, sl * D:(sl + 1) * D], AF.Identity,
                            accum_out=st1[:, i:i + 1]),
                        waits=[evl, evs], inc=c_s)
            evs = P.op("scalar",
                       lambda s, i=i, sl=sl: s.activation(
                           psA[:], xbuf[:, sl * D:(sl + 1) * D], AF.Square,
                           accum_out=sq1[:, i:i + 1]),
                       waits=[evid], inc=c_s)
            P.op("vector",
                 lambda v, i=i, sl=sl: v.reduce_max(
                     rmx[:, i:i + 1], xbuf[:, sl * D:(sl + 1) * D], axis=AX),
                 waits=[evl], inc=None)
            evv = P.op("vector",
                       lambda v, i=i, sl=sl: v.tensor_reduce(
                           rmn[:, i:i + 1], xbuf[:, sl * D:(sl + 1) * D], AX, ALU.min),
                       waits=[evl], inc=c_v)
            ev_use[sl] = [evs, evv]

        evm1 = P.op("vector", lambda v: v.tensor_scalar(m1[:], st1[:], 1.0 / D, None, ALU.mult),
                    waits=[evs, evv], inc=c_v)
        P.op("vector", lambda v: v.tensor_scalar(sq1[:], sq1[:], 1.0 / D, None, ALU.mult),
             inc=None)
        P.op("vector", lambda v: v.scalar_tensor_tensor(st1[:], m1[:], -1.0, m1[:],
                                                        ALU.mult, ALU.mult), inc=None)
        P.op("vector", lambda v: v.tensor_add(st1[:], sq1[:], st1[:]), inc=None)
        evv3 = P.op("vector", lambda v: v.tensor_scalar(st1[:], st1[:], EPS, None, ALU.add),
                    inc=c_v)
        evln = P.op("scalar", lambda s: s.activation(sq1[:], st1[:], AF.Ln),
                    waits=[evv3], inc=c_s)
        evr1 = P.op("scalar", lambda s: s.activation(sq1[:], sq1[:], AF.Exp, scale=-0.5),
                    waits=[evln], inc=c_s)
        P.op("vector", lambda v: v.tensor_sub(rmx[:], rmx[:], m1[:]), waits=[evm1], inc=None)
        P.op("vector", lambda v: v.tensor_sub(rmn[:], m1[:], rmn[:]), inc=None)
        P.op("vector", lambda v: v.tensor_max(rmx[:], rmx[:], rmn[:]), inc=None)
        P.op("vector", lambda v: v.tensor_mul(rmx[:], rmx[:], sq1[:]), waits=[evr1], inc=None)
        eva5 = P.op("vector", lambda v: v.reduce_max(sc[:, 1:2], rmx[:], axis=AX), inc=c_v)
        evgm = pmax_flip(sc[:, 1:2], scr[0:1, 2:3], [eva5])
        evrep = P.op("vector",
                     lambda v: v.tensor_scalar(flip_row[0:1, 0:8], onesrow[0:1, 0:8],
                                               scr[0:1, 2:3], None, ALU.mult),
                     waits=[evgm], inc=c_v)
        evci = S_GP.dma("gpsimd", lambda g: g.dma_start(out=cc1i.ap(), in_=flip_row[0:1, 0:8]),
                     waits=[evrep])
        evcc = P.op("gpsimd",
                    lambda g: g.collective_compute(
                        "AllReduce", ALU.max, replica_groups=[list(range(NCORE))],
                        ins=[cc1i.ap()], outs=[cc1o.ap()]),
                    waits=[evci], inc=c_cc)
        evco = S_GP.dma("gpsimd", lambda g: g.dma_start(out=flip_row[0:1, 0:8], in_=cc1o.ap()),
                     waits=[evcc])
        evis = P.op("vector", lambda v: v.tensor_scalar(scr[0:1, 3:4], flip_row[0:1, 0:1],
                                                        1.0 / 127.0, None, ALU.mult),
                    waits=[evco], inc=c_v)
        evrs = P.op("vector", lambda v: v.reciprocal(scr[0:1, 4:5], scr[0:1, 3:4]),
                    waits=[evis], inc=c_v)
        ev_rsb = bcast128(rs128[:, 0:1], scr[0:1, 4:5], [evrs])
        evch = P.op("vector", lambda v: v.tensor_mul(scr[0:1, 5:6], scr[0:1, 0:1],
                                                     scr[0:1, 3:4]),
                    waits=[evis, ev_ws2], inc=c_v)
        evch2 = P.op("vector", lambda v: v.tensor_scalar(scr[0:1, 5:6], scr[0:1, 5:6],
                                                         0.5, None, ALU.mult),
                     waits=[evch], inc=c_v)
        ev_chb = bcast128(ch128[:, 0:1], scr[0:1, 5:6], [evch2])
        evfb1a = P.op("vector", lambda v: v.reciprocal(scr[0:1, 12:13], scr[0:1, 5:6]),
                      waits=[evch2], inc=c_v)
        P.op("vector", lambda v: v.tensor_scalar(scr[0:1, 12:13], scr[0:1, 12:13],
                                                 0.5, None, ALU.mult), inc=None)
        ev_fb1 = P.op("vector",
                      lambda v: v.tensor_single_scalar(fbrow[0:1, 0:1024],
                                                       fbrow[0:1, 0:1024],
                                                       scr[0:1, 12:13], ALU.mult),
                      waits=[evfb1a, ev_const[5], ev_ones1b], inc=c_v)
        evr1s = P.op("vector",
                     lambda v: v.tensor_single_scalar(r1s[:], sq1[:], rs128[:, 0:1], ALU.mult),
                     waits=[ev_rsb, evr1], inc=c_v)
        evb1s = P.op("vector",
                     lambda v: v.scalar_tensor_tensor(b1s[:], m1[:], -1.0, r1s[:],
                                                      ALU.mult, ALU.mult),
                     waits=[evr1s], inc=c_v)
        evb1s = P.op("vector",
                     lambda v: v.tensor_single_scalar(b1s[:], b1s[:], 256.5, ALU.add),
                     inc=c_v)


        ev_acc_prev = evr1
        ev_wqs = P.op("vector",
                      lambda v: v.tensor_scalar(wq_sb[:], wq_sb[:], 0.125, None, ALU.mult),
                      waits=[ev_w[("wq", 7)]], inc=c_v)

        ev_tapx1 = [None] * 4
        ev_xfree = [evb1s, evb1s, [evb1s, ev_qinb_free, ev_wstage[2]],
                    [evb1s, ev_qinb_free, ev_wstage[3]]]
        ev_qtb_free = None
        ev_psA_free = None
        ev_psB_free = [ev_ws2]
        ev_free_CD = None
        ev_free_A01 = None
        ev_rsc_free = {0: None, 1: None}
        ev_pbuf_free = [None, None]
        ev_ktile = {}

        def ffn_mm(evq, fb_off=0, ev_fb=None, acc=None, acc_wait=None,
                   tkey="p2", tregions=None):
            nonlocal ev_psA_free, ev_qinb_free
            if acc is None:
                a0, a1 = psA[:, 0:512], psA[:, 512:1024]
            else:
                a0, a1 = acc
            aw = [ev_psA_free] if acc_wait is None else list(acc_wait)
            evqt = transpose_tile(tq, qint, 0, [evq], identf, ev_idf,
                                  key=tkey, regions=tregions)
            ev_qinb_free = evqt
            for kc in range(8):
                w = ([evqt] + aw) if kc == 0 else []
                P.op("tensor",
                     lambda t, kc=kc, a0=a0: t.matmul(
                         a0, qint[:, 128 * kc:128 * (kc + 1)],
                         qwt_sb[:, kc * D: kc * D + 512],
                         start=(kc == 0), stop=False),
                     waits=w, inc=None)
                P.op("tensor",
                     lambda t, kc=kc, a1=a1: t.matmul(
                         a1, qint[:, 128 * kc:128 * (kc + 1)],
                         qwt_sb[:, kc * D + 512: kc * D + 1024],
                         start=(kc == 0), stop=False),
                     waits=[], inc=None)
            P.op("tensor",
                 lambda t, fb_off=fb_off, a0=a0: t.matmul(
                     a0, ones1b[0:1, :], fbrow[0:1, fb_off:fb_off + 512],
                     start=False, stop=True),
                 waits=[ev_fb], inc=None)
            evmm = P.op("tensor",
                        lambda t, fb_off=fb_off, a1=a1: t.matmul(
                            a1, ones1b[0:1, :],
                            fbrow[0:1, fb_off + 512:fb_off + 1024],
                            start=False, stop=True),
                        waits=[], inc=c_t)
            return evmm

        def ffn_epi(sl, chcol, evmm, extra=(), src=None, xw=()):
            nonlocal ev_psA_free
            eve1 = P.op("scalar",
                        lambda s, chcol=chcol: s.activation(
                            epi[:], psA[:], AF.Identity, scale=chcol),
                        waits=[evmm, ev_chb] + list(extra), inc=c_s)
            ev_psA_free = eve1
            xs = xbuf[:, sl * D:(sl + 1) * D] if src is None else src
            evx1 = P.op("gpsimd",
                        lambda g_, sl=sl, xs=xs: g_.tensor_add(
                            xbuf[:, sl * D:(sl + 1) * D], epi[:], xs),
                        waits=[eve1] + list(xw), inc=c_g)
            return evx1

        def pre_quant0(gbase):
            evl0 = s_ld.dma("sync",
                            lambda s, gt=gbase: s.dma_start(
                                out=rsc2[:, 0:D],
                                in_=x_in[128 * gt:128 * (gt + 1), :]),
                            waits=[ev_rsc_free[0], ev_rsc_free[1], evb1s])
            return quant_tile(0, r1s[:, gbase:gbase + 1],
                              b1s[:, gbase:gbase + 1], [evl0], src=rsc2)

        ev_q0_next = None
        for s_i in range(NSEQ):
            ev_vproj = [None] * 16
            for g in range(NG):
                gbase = s_i * 16 + 4 * g
                ev_xn2t = [None] * 4
                ev_q = ev_q0_next if ev_q0_next is not None else pre_quant0(gbase)
                for tau in range(4):
                    gt = gbase + tau
                    sl = tau
                    evmm = ffn_mm(ev_q, 0, ev_fb1)
                    if tau < 3:
                        evl = s_ld.dma("sync",
                                       lambda s, gt=gt + 1, sl=sl + 1: s.dma_start(
                                           out=xbuf[:, sl * D:(sl + 1) * D],
                                           in_=x_in[128 * gt:128 * (gt + 1), :]),
                                       waits=[ev_xfree[sl + 1], evb1s])
                        ev_q = quant_tile(sl + 1, r1s[:, gt + 1:gt + 2],
                                          b1s[:, gt + 1:gt + 2], [evl])
                    if tau == 0:
                        evx1 = ffn_epi(sl, ch128[:, 0:1], evmm,
                                       src=rsc2[:, 0:D], xw=[ev_xfree[0]])
                        ev_rsc_free[0] = evx1
                        ev_rsc_free[1] = evx1
                    else:
                        evx1 = ffn_epi(sl, ch128[:, 0:1], evmm)
                    if "x1" in taps:
                        ev_tapx1[sl] = S_GP.dma(
                            "gpsimd", lambda g_, gt=gt, sl=sl: g_.dma_start(
                                out=taps["x1"][128 * gt:128 * (gt + 1), :],
                                in_=xbuf[:, sl * D:(sl + 1) * D]),
                            waits=[evx1])
                    ev21 = P.op("vector",
                                lambda v, sl=sl, tau=tau: v.reduce_sum(
                                    m2t[:, tau:tau + 1],
                                    xbuf[:, sl * D:(sl + 1) * D], axis=AX),
                                waits=[evx1], inc=c_v)
                    ev22 = P.op("scalar",
                                lambda s, sl=sl, tau=tau: s.activation(
                                    psB[:, 0:1024], xbuf[:, sl * D:(sl + 1) * D],
                                    AF.Square, accum_out=v2t[:, tau:tau + 1]),
                                waits=[evx1, ev_acc_prev], inc=c_s)
                    ev_acc_prev = ev22
                    P.op("vector",
                         lambda v, tau=tau: v.tensor_scalar(
                             m2t[:, tau:tau + 1], m2t[:, tau:tau + 1], 1.0 / D, None,
                             ALU.mult),
                         waits=[ev22, ev21], inc=None)
                    P.op("vector",
                         lambda v, tau=tau: v.tensor_scalar(
                             v2t[:, tau:tau + 1], v2t[:, tau:tau + 1], 1.0 / D, None,
                             ALU.mult), inc=None)
                    P.op("vector",
                         lambda v, tau=tau: v.scalar_tensor_tensor(
                             r2t[:, tau:tau + 1], m2t[:, tau:tau + 1], -1.0,
                             m2t[:, tau:tau + 1], ALU.mult, ALU.mult), inc=None)
                    P.op("vector",
                         lambda v, tau=tau: v.tensor_add(
                             v2t[:, tau:tau + 1], v2t[:, tau:tau + 1],
                             r2t[:, tau:tau + 1]), inc=None)
                    ev2v = P.op("vector",
                                lambda v, tau=tau: v.tensor_scalar(
                                    v2t[:, tau:tau + 1], v2t[:, tau:tau + 1], EPS, None,
                                    ALU.add), inc=c_v)
                    ev2l = P.op("scalar",
                                lambda s, tau=tau: s.activation(
                                    r2t[:, tau:tau + 1], v2t[:, tau:tau + 1], AF.Ln),
                                waits=[ev2v], inc=c_s)
                    ev2r = P.op("scalar",
                                lambda s, tau=tau: s.activation(
                                    r2t[:, tau:tau + 1], r2t[:, tau:tau + 1], AF.Exp,
                                    scale=-0.5),
                                waits=[ev2l], inc=c_s)
                    ev2b = P.op("vector",
                                lambda v, tau=tau: v.scalar_tensor_tensor(
                                    b2t[:, tau:tau + 1], m2t[:, tau:tau + 1], -1.0,
                                    r2t[:, tau:tau + 1], ALU.mult, ALU.mult),
                                waits=[ev2r], inc=c_v)
                    evn2 = P.op("scalar",
                                lambda s, sl=sl, tau=tau: s.activation(
                                    epi[:], xbuf[:, sl * D:(sl + 1) * D], AF.Identity,
                                    scale=r2t[:, tau:tau + 1],
                                    bias=b2t[:, tau:tau + 1]),
                                waits=[ev2b], inc=c_s)
                    evnt = transpose_tile(epi, xn2t, tau * D, [evn2], identf, ev_idf)
                    ev_xn2t[tau] = evnt
                    j = 4 * g + tau
                    evvm = None
                    for kc in range(8):
                        w = [evnt, ev_psA_free] if kc == 0 else []
                        P.op("tensor",
                             lambda t, kc=kc, tau=tau: t.matmul(
                                 psA[:, 0:512],
                                 xn2t[:, tau * D + 128 * kc: tau * D + 128 * (kc + 1)],
                                 wv_sb[:, kc * D: kc * D + 512],
                                 start=(kc == 0), stop=(kc == 7)),
                             waits=w + ([ev_w[("wv", 7)]] if kc == 0 else []), inc=None)
                        evvm = P.op("tensor",
                                    lambda t, kc=kc, tau=tau: t.matmul(
                                        psA[:, 512:1024],
                                        xn2t[:, tau * D + 128 * kc: tau * D + 128 * (kc + 1)],
                                        wv_sb[:, kc * D + 512: kc * D + 1024],
                                        start=(kc == 0), stop=(kc == 7)),
                                    waits=[], inc=(c_t if kc == 7 else None))
                    evve = P.op("scalar",
                                lambda sc_, j=j: sc_.copy(vtb[:, j * D:(j + 1) * D],
                                                          psA[:]),
                                waits=[evvm], inc=c_s)
                    ev_psA_free = evve
                    ev_vproj[j] = evve

                def proj_rope(w_sb, dst_sb, dst_off, oc, wa, par):
                    acc = psB[:, 1024 * par: 1024 * par + 512]
                    swp = psB[:, 1024 * par + 512: 1024 * (par + 1)]
                    evpm = None
                    for kc in range(8):
                        w = (list(wa) + ev_proj_free[par]) if kc == 0 else []
                        evpm = P.op("tensor",
                                    lambda t, kc=kc, oc=oc, w_sb=w_sb, acc=acc: t.matmul(
                                        acc,
                                        w_sb[:, kc * D + 128 * oc: kc * D + 128 * (oc + 1)],
                                        xn2t[:].rearrange("p (tl d) -> p tl d", tl=4)[
                                            :, :, 128 * kc:128 * (kc + 1)],
                                        start=(kc == 0), stop=(kc == 7)),
                                    waits=w, inc=(c_t if kc == 7 else None))
                    evx = P.op("vector",
                               lambda v, acc=acc: v.tensor_copy(ropex, acc),
                               waits=[evpm, ev_ropex_free[0]], inc=c_v)
                    evsw = P.op("tensor",
                                lambda t, swp=swp: t.matmul(swp, swapt[:], ropex,
                                                            start=True, stop=True),
                                waits=[evx, ev_const[0]], inc=c_t)
                    evt1 = P.op("gpsimd",
                                lambda g_, dst_sb=dst_sb, dst_off=dst_off, g=g:
                                g_.tensor_mul(
                                    dst_sb[:, dst_off:dst_off + 512], ropex,
                                    ropec[:, 512 * g:512 * (g + 1)]),
                                waits=[ev_const[2], evx], inc=c_g)
                    evt2 = P.op("vector",
                                lambda v, g=g, swp=swp: v.tensor_mul(
                                    ropet2, swp,
                                    ropes[:, 512 * g:512 * (g + 1)]),
                                waits=[evsw, ev_const[3], ev_ropex_free[1]], inc=c_v)
                    evo = P.op("gpsimd",
                               lambda g_, dst_sb=dst_sb, dst_off=dst_off: g_.tensor_add(
                                   dst_sb[:, dst_off:dst_off + 512],
                                   dst_sb[:, dst_off:dst_off + 512], ropet2),
                               waits=[evt1, evt2], inc=c_g)
                    ev_ropex_free[0] = evt1
                    ev_ropex_free[1] = evo
                    ev_proj_free[par] = [evo]
                    return evo

                ev_qrope = []
                ev_krope = []
                ev_proj_free = {0: ev_psB_free, 1: ev_psB_free}
                ev_ropex_free = [None, None]
                for oc in range(8):
                    wa = ev_xn2t + [ev_wqs, ev_qtb_free]
                    ev_qrope.append(proj_rope(wq_sb, qtb, 512 * oc, oc, wa, 0))
                    wa2 = ev_xn2t + [ev_w[("wk", 7)], ev_qtb_free]
                    ev_krope.append(proj_rope(wk_sb, ktb, SEQ * oc + 512 * g, oc, wa2, 1))
                ev_ktile[(s_i, g)] = ev_krope
                ev_psB_free = ev_proj_free[0] + ev_proj_free[1]

                nk = 4 * (g + 1)
                ev_att_done = [None] * 8
                last_exps = []
                evrl_last = None
                pend_av = None
                pend_epi = None
                for hp in range(8):
                    par = hp % 2
                    if par == 0:
                        avA, avB = psC[0:64, :], psD[64:128, :]
                        smA, smB = psA[0:64, 0:512], psA[64:128, 512:1024]
                    else:
                        avA, avB = psA[0:64, 0:512], psA[64:128, 512:1024]
                        smA, smB = psC[0:64, :], psD[64:128, :]
                    evav = None
                    ev_av4 = None
                    exps = []
                    for j in range(nk):
                        sloc = 1024 * (j % 2)
                        w = []
                        if len(exps) >= 2:
                            w = [exps[-2]]
                        elif last_exps:
                            w = [last_exps[-1]]
                        else:
                            w = list(ev_psB_free)
                        if j == 0 and hp == 0:
                            w += [ev_qrope[7], ev_krope[7]]
                            for gg in range(g):
                                w.append(ev_ktile[(s_i, gg)][hp])
                        if j == 0:
                            w += [ev_qrope[hp], ev_krope[hp]]
                        P.op("tensor",
                             lambda t, j=j, hp=hp, sloc=sloc: t.matmul(
                                 psB[:, sloc: sloc + 512],
                                 ktb[0:64, SEQ * hp + 128 * j: SEQ * hp + 128 * (j + 1)],
                                 qtb[0:64, 512 * hp:512 * (hp + 1)],
                                 start=True, stop=True, tile_position=(0, 0)),
                             waits=w, inc=None)
                        evqk = P.op("tensor",
                                    lambda t, j=j, hp=hp, sloc=sloc: t.matmul(
                                        psB[:, sloc + 512: sloc + 1024],
                                        ktb[64:128,
                                            SEQ * hp + 128 * j: SEQ * hp + 128 * (j + 1)],
                                        qtb[64:128, 512 * hp:512 * (hp + 1)],
                                        start=True, stop=True, tile_position=(64, 0)),
                                    waits=[], inc=c_t)
                        pb = j % 2
                        evex = P.op("scalar",
                                    lambda s, pb=pb, sloc=sloc: s.activation(
                                        pbuf[:, pb * D:(pb + 1) * D],
                                        psB[:, sloc:sloc + 1024], AF.Exp),
                                    waits=[evqk, ev_pbuf_free[pb]], inc=c_s)
                        exps.append(evex)
                        r = j - 4 * g
                        evfix = evex
                        if r >= 0:
                            if r > 0:
                                P.op("vector",
                                     lambda v, pb=pb, r=r: v.memset(
                                         pbuf[:, pb * D: pb * D + 128 * r], 0.0),
                                     waits=[evex], inc=None)
                                P.op("vector",
                                     lambda v, pb=pb, r=r: v.memset(
                                         pbuf[:, pb * D + 512: pb * D + 512 + 128 * r], 0.0),
                                     waits=[evex], inc=None)
                            P.op("vector",
                                 lambda v, pb=pb, r=r: v.tensor_mul(
                                     pbuf[:, pb * D + 128 * r: pb * D + 128 * (r + 1)],
                                     pbuf[:, pb * D + 128 * r: pb * D + 128 * (r + 1)],
                                     trit[:]),
                                 waits=[evex, ev_const[1]], inc=None)
                            evfix = P.op("vector",
                                         lambda v, pb=pb, r=r: v.tensor_mul(
                                             pbuf[:, pb * D + 512 + 128 * r:
                                                  pb * D + 512 + 128 * (r + 1)],
                                             pbuf[:, pb * D + 512 + 128 * r:
                                                  pb * D + 512 + 128 * (r + 1)],
                                             trit[:]),
                                         waits=[evex], inc=c_v)
                        if pend_av is not None:
                            pend_av()
                            pend_av = None
                        if pend_epi is not None:
                            pend_epi()
                            pend_epi = None
                        wav = [evfix, ev_vproj[j], ev_ones]
                        if j == 0:
                            if par == 0:
                                wav += [ev_free_CD, ev_free_A01]
                            else:
                                wav += [ev_free_A01, ev_free_CD]
                        last = (j == nk - 1)

                        def rec_av(j=j, hp=hp, pb=pb, last=last, wav=wav,
                                   avA=avA, avB=avB, smA=smA, smB=smB):
                            nonlocal evav, ev_av4
                            evAVa = P.op("tensor",
                                         lambda t, j=j, hp=hp, pb=pb, last=last, avA=avA:
                                         t.matmul(
                                             avA,
                                             vtb[:, j * D + 128 * hp: j * D + 128 * hp + 64],
                                             pbuf[:, pb * D: pb * D + 512],
                                             start=(j == 0), stop=last,
                                             tile_position=(0, 0)),
                                         waits=wav, inc=(c_t if last else None))
                            evAVb = P.op("tensor",
                                         lambda t, j=j, hp=hp, pb=pb, last=last, avB=avB:
                                         t.matmul(
                                             avB,
                                             vtb[:, j * D + 128 * hp + 64:
                                                 j * D + 128 * (hp + 1)],
                                             pbuf[:, pb * D + 512: pb * D + 1024],
                                             start=(j == 0), stop=last,
                                             tile_position=(0, 64)),
                                         waits=[], inc=(c_t if last else None))
                            evSa = P.op("tensor",
                                        lambda t, j=j, pb=pb, last=last, smA=smA: t.matmul(
                                            smA, ones64[:, 0:64],
                                            pbuf[:, pb * D: pb * D + 512],
                                            start=(j == 0), stop=last,
                                            tile_position=(0, 0)),
                                        waits=[], inc=(c_t if last else None))
                            evav = P.op("tensor",
                                        lambda t, j=j, pb=pb, last=last, smB=smB: t.matmul(
                                            smB, ones64[:, 0:64],
                                            pbuf[:, pb * D + 512: pb * D + 1024],
                                            start=(j == 0), stop=last,
                                            tile_position=(0, 64)),
                                        waits=[], inc=c_t)
                            if last:
                                ev_av4 = [evAVa, evAVb, evSa, evav]
                            ev_pbuf_free[pb] = evav
                        pend_av = rec_av
                    last_exps = exps

                    def rec_epi(hp=hp, par=par, avA=avA, avB=avB, smA=smA, smB=smB):
                        nonlocal ev_free_CD, ev_free_A01, evrl_last
                        evacA = P.op("vector",
                                     lambda v, hp=hp, avA=avA: v.tensor_copy(
                                         att[0:64, 512 * hp:512 * (hp + 1)], avA),
                                     waits=ev_av4, inc=c_v)
                        evacB = P.op("vector",
                                     lambda v, hp=hp, avB=avB: v.tensor_copy(
                                         att[64:128, 512 * hp:512 * (hp + 1)], avB),
                                     waits=[], inc=c_v)
                        evrlA = P.op("scalar",
                                     lambda s, smA=smA, hp=hp: s.activation(
                                         rsc2[0:64, 512 * (hp % 2):512 * (hp % 2) + 512],
                                         smA, AF.Ln),
                                     waits=ev_av4 + [ev_rsc_free[par]], inc=c_s)
                        evrl = P.op("scalar",
                                    lambda s, smB=smB, hp=hp: s.activation(
                                        rsc2[64:128, 512 * (hp % 2):512 * (hp % 2) + 512],
                                        smB, AF.Ln),
                                    waits=[evrlA], inc=c_s)
                        evre = P.op("scalar",
                                    lambda s, hp=hp: s.activation(
                                        rsc2[:, 512 * (hp % 2):512 * (hp % 2) + 512],
                                        rsc2[:, 512 * (hp % 2):512 * (hp % 2) + 512],
                                        AF.Exp, scale=-1.0),
                                    waits=[evrl], inc=c_s)
                        evatA = P.op("vector",
                                     lambda v, hp=hp: v.tensor_mul(
                                         att[0:64, 512 * hp:512 * (hp + 1)],
                                         att[0:64, 512 * hp:512 * (hp + 1)],
                                         rsc2[0:64, 512 * (hp % 2):512 * (hp % 2) + 512]),
                                     waits=[evre], inc=c_v)
                        evat = P.op("vector",
                                    lambda v, hp=hp: v.tensor_mul(
                                        att[64:128, 512 * hp:512 * (hp + 1)],
                                        att[64:128, 512 * hp:512 * (hp + 1)],
                                        rsc2[64:128, 512 * (hp % 2):512 * (hp % 2) + 512]),
                                    waits=[evatA], inc=c_v)
                        ev_rsc_free[par] = evat
                        if par == 0:
                            ev_free_CD = evacB
                            ev_free_A01 = evrl
                        else:
                            ev_free_A01 = evacB
                            ev_free_CD = evrl
                        ev_att_done[hp] = evat
                        evrl_last = evrl
                    pend_epi = rec_epi
                pend_av()
                pend_epi()
                ev_qtb_free = ev_att_done[7]
                if not (s_i == NSEQ - 1 and g == NG - 1):
                    nxt = gbase + 4
                    ev_q0_next = pre_quant0(nxt)
                ev_psB_free = [last_exps[-1], last_exps[-2] if len(last_exps) > 1 else None]

                if "attn" in taps and s_i == 0:
                    for hp in range(8):
                        for hf in range(2):
                            s_gl.dma("gpsimd",
                                     lambda g_, hp=hp, g2=g, hf=hf: g_.dma_start(
                                         out=taps["attn"][128 * hp + 64 * hf:
                                                          128 * hp + 64 * (hf + 1),
                                                          512 * g2:512 * (g2 + 1)],
                                         in_=att[64 * hf:64 * (hf + 1),
                                                 512 * hp:512 * (hp + 1)]),
                                     waits=[ev_att_done[hp]])

                ev_oproj_free = {0: None, 1: None}
                for tau in range(4):
                    gt = gbase + tau
                    sl = tau
                    if tau % 2 == 0:
                        oA, oB = psA[:, 0:512], psA[:, 512:1024]
                    else:
                        oA, oB = psC[:], psD[:]
                    evom = None
                    for hp in range(8):
                        if hp == 0:
                            if tau == 0:
                                w = ev_att_done + [ev_psA_free, ev_free_A01,
                                                   ev_free_CD, ev_w[("wo", 7)]]
                            elif tau == 1:
                                w = [ev_free_A01, ev_free_CD]
                            else:
                                w = [ev_oproj_free[tau % 2]]
                        else:
                            w = []
                        P.op("tensor",
                             lambda t, hp=hp, tau=tau, oA=oA: t.matmul(
                                 oA,
                                 att[:, 512 * hp + 128 * tau: 512 * hp + 128 * (tau + 1)],
                                 wo_sb[:, hp * D: hp * D + 512],
                                 start=(hp == 0), stop=False),
                             waits=w, inc=None)
                        P.op("tensor",
                             lambda t, hp=hp, tau=tau, oB=oB: t.matmul(
                                 oB,
                                 att[:, 512 * hp + 128 * tau:
                                      512 * hp + 128 * (tau + 1)],
                                 wo_sb[:, hp * D + 512: hp * D + 1024],
                                 start=(hp == 0), stop=False),
                             waits=[], inc=None)
                    P.op("tensor",
                         lambda t, oA=oA: t.matmul(oA, ones1b[0:1, :],
                                                   fbrow[0:1, 1024:1536],
                                                   start=False, stop=True),
                         waits=[ev_const[4], ev_ones1b], inc=None)
                    evom = P.op("tensor",
                                lambda t, oB=oB: t.matmul(oB, ones1b[0:1, :],
                                                          fbrow[0:1, 1536:2048],
                                                          start=False, stop=True),
                                waits=[], inc=c_t)
                    if tau % 2 == 0:
                        evx2 = P.op("vector",
                                    lambda v, sl=sl: v.tensor_add(
                                        xbuf[:, sl * D:(sl + 1) * D], psA[:],
                                        xbuf[:, sl * D:(sl + 1) * D]),
                                    waits=[evom, ev_tapx1[sl]], inc=c_v)
                        ev_psA_free = evx2
                    else:
                        P.op("vector",
                             lambda v, sl=sl: v.tensor_add(
                                 xbuf[:, sl * D:sl * D + 512], psC[:],
                                 xbuf[:, sl * D:sl * D + 512]),
                             waits=[evom, ev_tapx1[sl]], inc=None)
                        evx2 = P.op("vector",
                                    lambda v, sl=sl: v.tensor_add(
                                        xbuf[:, sl * D + 512:(sl + 1) * D], psD[:],
                                        xbuf[:, sl * D + 512:(sl + 1) * D]),
                                    waits=[], inc=c_v)
                    ev_oproj_free[tau % 2] = evx2
                    evid3 = P.op("vector",
                                 lambda v, sl=sl, gt=gt: v.reduce_sum(
                                     st3[:, gt:gt + 1],
                                     xbuf[:, sl * D:(sl + 1) * D], axis=AX),
                                 waits=[evx2], inc=c_v)
                    evv3b = P.op("vector",
                                 lambda v, sl=sl, gt=gt: v.tensor_reduce(
                                     rmn3[:, gt:gt + 1], xbuf[:, sl * D:(sl + 1) * D],
                                     AX, ALU.min),
                                 waits=[], inc=c_v)
                    evs3 = P.op("scalar",
                                lambda s, sl=sl, gt=gt: s.activation(
                                    psB[:, 0:1024], xbuf[:, sl * D:(sl + 1) * D],
                                    AF.Square, accum_out=sq3[:, gt:gt + 1]),
                                waits=[evx2, ev_acc_prev], inc=c_s)
                    ev_acc_prev = evs3
                    evmx3 = P.op("vector",
                                 lambda v, sl=sl, gt=gt: v.reduce_max(
                                     rmx3[:, gt:gt + 1], xbuf[:, sl * D:(sl + 1) * D],
                                     axis=AX),
                                 waits=[evx2], inc=c_v)
                    evst = s_st.dma("scalar",
                                    lambda s_, gt=gt, sl=sl: s_.dma_start(
                                        out=x2_h[128 * gt:128 * (gt + 1), :],
                                        in_=xbuf[:, sl * D:(sl + 1) * D]),
                                    waits=[evx2])
                    if "x2" in taps:
                        evst = S_GP.dma("gpsimd",
                                        lambda g_, gt=gt, sl=sl: g_.dma_start(
                                            out=taps["x2"][128 * gt:128 * (gt + 1), :],
                                            in_=xbuf[:, sl * D:(sl + 1) * D]),
                                        waits=[evst])
                    ev_xfree[sl] = [evst, evid3, evv3b, evs3, evmx3]
                ev_free_A01 = ev_oproj_free[0]
                ev_free_CD = ev_oproj_free[1]

        if "kt" in taps:
            for hp in range(8):
                S_GP.dma("gpsimd",
                      lambda g_, hp=hp: g_.dma_start(
                          out=taps["kt"][128 * hp:128 * (hp + 1), :],
                          in_=ktb[:, SEQ * hp:SEQ * (hp + 1)]),
                      waits=[ev_ktile[(NSEQ - 1, NG - 1)][hp]])
        if "qt" in taps:
            for hp in range(8):
                S_GP.dma("gpsimd",
                      lambda g_, hp=hp: g_.dma_start(
                          out=taps["qt"][128 * hp:128 * (hp + 1), 1536:2048],
                          in_=qtb[:, 512 * hp:512 * (hp + 1)]),
                      waits=[ev_qrope[hp]])

        ev_allst = s_st.all_done()
        psT_free["p4o"] = ev_free_CD
        evm3 = P.op("vector", lambda v: v.tensor_scalar(m3[:], st3[:], 1.0 / D, None,
                                                        ALU.mult),
                    waits=list(ev_xfree), inc=c_v)
        P.op("vector", lambda v: v.tensor_scalar(sq3[:], sq3[:], 1.0 / D, None, ALU.mult),
             inc=None)
        P.op("vector", lambda v: v.scalar_tensor_tensor(st3[:], m3[:], -1.0, m3[:],
                                                        ALU.mult, ALU.mult), inc=None)
        P.op("vector", lambda v: v.tensor_add(st3[:], sq3[:], st3[:]), inc=None)
        evv3c = P.op("vector", lambda v: v.tensor_scalar(st3[:], st3[:], EPS, None, ALU.add),
                     inc=c_v)
        evln3 = P.op("scalar", lambda s: s.activation(sq3[:], st3[:], AF.Ln),
                     waits=[evv3c], inc=c_s)
        evr3 = P.op("scalar", lambda s: s.activation(sq3[:], sq3[:], AF.Exp, scale=-0.5),
                    waits=[evln3], inc=c_s)
        P.op("vector", lambda v: v.tensor_sub(rmx3[:], rmx3[:], m3[:]), waits=[evm3],
             inc=None)
        P.op("vector", lambda v: v.tensor_sub(rmn3[:], m3[:], rmn3[:]), inc=None)
        P.op("vector", lambda v: v.tensor_max(rmx3[:], rmx3[:], rmn3[:]), inc=None)
        P.op("vector", lambda v: v.tensor_mul(rmx3[:], rmx3[:], sq3[:]), waits=[evr3],
             inc=None)
        eva5b = P.op("vector", lambda v: v.reduce_max(sc[:, 2:3], rmx3[:], axis=AX), inc=c_v)
        evgm2 = pmax_flip(sc[:, 2:3], scr[0:1, 8:9], [eva5b])
        evrep2 = P.op("vector",
                      lambda v: v.tensor_scalar(flip_row[0:1, 0:8], onesrow[0:1, 0:8],
                                                scr[0:1, 8:9], None, ALU.mult),
                      waits=[evgm2], inc=c_v)
        evci2 = S_GP.dma("gpsimd", lambda g: g.dma_start(out=cc2i.ap(), in_=flip_row[0:1, 0:8]),
                      waits=[evrep2])
        evcc2 = P.op("gpsimd",
                     lambda g: g.collective_compute(
                         "AllReduce", ALU.max, replica_groups=[list(range(NCORE))],
                         ins=[cc2i.ap()], outs=[cc2o.ap()]),
                     waits=[evci2, evcc], inc=c_cc)
        evco2 = S_GP.dma("gpsimd", lambda g: g.dma_start(out=flip_row[0:1, 0:8], in_=cc2o.ap()),
                      waits=[evcc2])
        evis2 = P.op("vector", lambda v: v.tensor_scalar(scr[0:1, 9:10], flip_row[0:1, 0:1],
                                                         1.0 / 127.0, None, ALU.mult),
                     waits=[evco2], inc=c_v)
        evrs2 = P.op("vector", lambda v: v.reciprocal(scr[0:1, 10:11], scr[0:1, 9:10]),
                     waits=[evis2], inc=c_v)
        ev_rsb2 = bcast128(rs2128[:, 0:1], scr[0:1, 10:11], [evrs2])
        evch_ = P.op("vector", lambda v: v.tensor_mul(scr[0:1, 11:12], scr[0:1, 0:1],
                                                      scr[0:1, 9:10]),
                     waits=[evis2], inc=c_v)
        evch2_ = P.op("vector", lambda v: v.tensor_scalar(scr[0:1, 11:12], scr[0:1, 11:12],
                                                          0.5, None, ALU.mult),
                      waits=[evch_], inc=c_v)
        ev_chb2 = bcast128(ch2128[:, 0:1], scr[0:1, 11:12], [evch2_])
        evfb2a = P.op("vector", lambda v: v.reciprocal(scr[0:1, 13:14], scr[0:1, 11:12]),
                      waits=[evch2_], inc=c_v)
        P.op("vector", lambda v: v.tensor_mul(scr[0:1, 13:14], scr[0:1, 13:14],
                                              scr[0:1, 5:6]), inc=None)
        ev_fb2 = P.op("vector",
                      lambda v: v.tensor_single_scalar(fbrow[0:1, 1024:2048],
                                                       fbrow[0:1, 0:1024],
                                                       scr[0:1, 13:14], ALU.mult),
                      waits=[evfb2a, (c_t, c_t.n)], inc=c_v)
        evr3s = P.op("vector",
                     lambda v: v.tensor_single_scalar(r3s[:], sq3[:], rs2128[:, 0:1],
                                                      ALU.mult),
                     waits=[ev_rsb2, evr3], inc=c_v)
        evb3s = P.op("vector",
                     lambda v: v.scalar_tensor_tensor(b3s[:], m3[:], -1.0, r3s[:],
                                                      ALU.mult, ALU.mult),
                     waits=[evr3s], inc=c_v)
        evb3s = P.op("vector",
                     lambda v: v.tensor_single_scalar(b3s[:], b3s[:], 256.5, ALU.add),
                     inc=c_v)

        ev_epi_free = {0: None, 1: None}
        ev_acc_free = {0: [ev_psA_free], 1: None}
        ev_p4free = [evb3s, evb3s, evb3s, evb3s]
        ev_last_out = None

        def p4_load(gt):
            sl = gt % 4
            e1 = s_ld.dma("sync",
                          lambda s, gt=gt, sl=sl: s.dma_start(
                              out=xbuf[:, sl * D:(sl + 1) * D],
                              in_=x2_h[128 * gt:128 * (gt + 1), :]),
                          waits=[ev_p4free[sl], ev_allst])
            return [e1]

        evls = {0: p4_load(0), 1: p4_load(1), 2: p4_load(2)}
        ev_q = quant_tile(0, r3s[:, 0:1], b3s[:, 0:1], evls[0])
        for gt in range(NT):
            sl = gt % 4
            par = gt % 2
            if par == 0:
                acc = (psA[:, 0:512], psA[:, 512:1024])
                tkey, tregions = "p2", None
            else:
                acc = (psB[:, 0:512], psB[:, 512:1024])
                tkey, tregions = "p4o", (psC, psD)
            evmm = ffn_mm(ev_q, 1024, ev_fb2, acc=acc,
                          acc_wait=[ev_acc_free[par]], tkey=tkey, tregions=tregions)
            if gt + 3 < NT:
                evls[gt + 3] = p4_load(gt + 3)
            if gt < NT - 1:
                ev_q = quant_tile((gt + 1) % 4, r3s[:, gt + 1:gt + 2],
                                  b3s[:, gt + 1:gt + 2], evls[gt + 1])
            ebuf = epi if par == 0 else rsc2
            if par == 0:
                eve1 = P.op("scalar",
                            lambda s: s.activation(epi[:], psA[:], AF.Identity,
                                                   scale=ch2128[:, 0:1]),
                            waits=[evmm, ev_chb2, ev_epi_free[0]], inc=c_s)
            else:
                P.op("scalar",
                     lambda s: s.activation(rsc2[:, 0:512], psB[:, 0:512],
                                            AF.Identity, scale=ch2128[:, 0:1]),
                     waits=[evmm, ev_chb2, ev_epi_free[1]], inc=c_s)
                eve1 = P.op("scalar",
                            lambda s: s.activation(rsc2[:, 512:1024],
                                                   psB[:, 512:1024],
                                                   AF.Identity, scale=ch2128[:, 0:1]),
                            waits=[], inc=c_s)
            ev_acc_free[par] = eve1
            evo = P.op("gpsimd",
                       lambda g_, sl=sl, ebuf=ebuf: g_.tensor_add(
                           ebuf[:, 0:1024], ebuf[:, 0:1024],
                           xbuf[:, sl * D:(sl + 1) * D]),
                       waits=[eve1], inc=c_g)
            evs1 = s_st.dma("scalar",
                            lambda s_, gt=gt, ebuf=ebuf: s_.dma_start(
                                out=out_h[128 * gt:128 * (gt + 1), :],
                                in_=ebuf[:, 0:1024]),
                            waits=[evo])
            ev_epi_free[par] = evs1
            ev_p4free[sl] = evo
            ev_last_out = evs1

        P.op("gpsimd", lambda g: g.nop(), waits=[ev_last_out], inc=None)

        P.emit(block)
    return nc


def _host_constants():
    HD = 64
    inv = 1.0 / (10000.0 ** (np.arange(0, HD, 2)[: HD // 2].astype(np.float64) / HD))
    ang = np.outer(np.arange(SEQ, dtype=np.float64), inv)
    cos = np.cos(ang).astype(np.float32)
    sin = np.sin(ang).astype(np.float32)
    ropec = np.concatenate([cos.T] * 4, axis=0).astype(ml_dtypes.bfloat16)
    ropes = np.concatenate([-sin.T, sin.T, -sin.T, sin.T], axis=0).astype(ml_dtypes.bfloat16)
    ident = np.eye(128, dtype=ml_dtypes.bfloat16)
    permsrc = np.concatenate([np.arange(32, 64), np.arange(0, 32),
                              np.arange(96, 128), np.arange(64, 96)])
    swap = np.zeros((128, 128), dtype=ml_dtypes.bfloat16)
    swap[permsrc, np.arange(128)] = 1
    tri = np.triu(np.ones((128, 128), dtype=np.float32)).astype(ml_dtypes.bfloat16)
    perm = np.zeros(D, dtype=np.int64)
    for c in range(8):
        base = 128 * c
        hA, hB = 2 * c, 2 * c + 1
        perm[base + 0:base + 32] = 64 * hA + 2 * np.arange(32)
        perm[base + 32:base + 64] = 64 * hA + 2 * np.arange(32) + 1
        perm[base + 64:base + 96] = 64 * hB + 2 * np.arange(32)
        perm[base + 96:base + 128] = 64 * hB + 2 * np.arange(32) + 1
    return ropec, ropes, ident, swap, tri, perm


_NC_CACHE = {}


def kernel(L0_attn_x, L0_attn_wq, L0_attn_wk, L0_attn_wv, L0_attn_wo_w,
           L0_attn_wo_b, L0_ffn_weight, L0_ffn_bias, _debug_taps=()):
    x = np.ascontiguousarray(np.asarray(L0_attn_x, dtype=np.float32))
    wq = np.asarray(L0_attn_wq, np.float32)
    wk = np.asarray(L0_attn_wk, np.float32)
    wv = np.ascontiguousarray(np.asarray(L0_attn_wv, np.float32))
    wo = np.ascontiguousarray(np.asarray(L0_attn_wo_w, np.float32))
    wob = np.asarray(L0_attn_wo_b, np.float32)
    fw = np.ascontiguousarray(np.asarray(L0_ffn_weight, np.float32))
    ffb = np.asarray(L0_ffn_bias, np.float32)

    ropec, ropes, ident, swap, tri, perm = _host_constants()
    wq_p = np.ascontiguousarray(wq[:, perm])
    wk_p = np.ascontiguousarray(wk[:, perm])
    wob_t = np.ascontiguousarray(
        np.broadcast_to(wob[None, :], (128, D))).astype(ml_dtypes.bfloat16)
    ffb_t = np.ascontiguousarray(
        np.broadcast_to(ffb[None, :], (128, D))).astype(ml_dtypes.bfloat16)

    key = tuple(_debug_taps)
    if key not in _NC_CACHE:
        _NC_CACHE[key] = build_program(_debug_taps)
    nc = _NC_CACHE[key]

    in_maps = []
    for c in range(NCORE):
        xs = np.ascontiguousarray(x[4 * c:4 * c + 4].reshape(ROWS, D))
        in_maps.append({
            "x_in": xs, "wq_h": wq_p, "wk_h": wk_p, "wv_h": wv, "wo_h": wo,
            "fw_h": fw, "wob_h": wob_t, "ffb_h": ffb_t,
            "ident_h": ident, "identf_h": ident.astype(np.float32),
            "swap_h": swap, "tri_h": tri,
            "ropec_h": ropec, "ropes_h": ropes,
        })
    res = run_bass_kernel_spmd(nc, in_maps, core_ids=list(range(NCORE)))
    out = np.stack([np.asarray(r["out_h"]).reshape(NSEQ, SEQ, D) for r in res.results])
    out = np.ascontiguousarray(out.reshape(32, SEQ, D))
    if _debug_taps:
        return out, res
    return out
